# version 01
# baseline (speedup 1.0000x reference)
import numpy as np
import ml_dtypes
from contextlib import ExitStack

import concourse.bass as bass
import concourse.mybir as mybir
from concourse.bass_utils import run_bass_kernel_spmd

F32 = mybir.dt.float32
BF16 = mybir.dt.bfloat16
AF = mybir.ActivationFunctionType
ALU = mybir.AluOpType
AX = mybir.AxisListType

MAGIC = float(np.float32(2 ** 23 + 2 ** 22))
T, D, HID = 4096, 1024, 4096
L, NCH = 512, 8
SLAB, NSLAB = 256, 16
NT = T // 128
NSEQ = 2
G64 = 0.99
EPS = 1e-5

DBG = False
PHASES = ("P0", "A", "B1", "B2a", "B2b")
A_STAGES = 6


class Ctr:

    def __init__(self, nc, ctx, name):
        self.h = ctx.enter_context(nc.semaphore(name))
        self.v = 0

    def inc(self, inst, n=1):
        inst.then_inc(self.h, n)
        self.v += n
        return self.v

    def wait(self, eng, v=None):
        if v is None:
            v = self.v
        if v <= 0:
            return
        assert v <= self.v, f"forward wait {v} > {self.v}"
        eng.wait_ge(self.h, v)


def _gp(e):
    return np.float32(G64 ** np.float64(e))


def build_consts():
    gamq = np.zeros((128, L), dtype=np.float32)
    for u in range(L):
        gamq[:, u] = _gp(u + 1)
    msk = np.zeros((128, 4, L), dtype=np.float32)
    for i in range(4):
        for p in range(128):
            s = 128 * i + p
            msk[p, i, s:] = _gp(-(s + 1))
    gamk = np.zeros((128, 4), dtype=np.float32)
    for i in range(4):
        for p in range(128):
            gamk[p, i] = _gp(511 - (128 * i + p))
    return gamq, msk, gamk


def build_nc():
    nc = bass.Bass("TRN2", target_bir_lowering=False)
    K = "ExternalOutput" if DBG else "Internal"

    x_in = nc.dram_tensor("x_in", [NSEQ, T, D], F32, kind="ExternalInput").ap()
    wq_in = nc.dram_tensor("wq_in", [D, D], F32, kind="ExternalInput").ap()
    wk_in = nc.dram_tensor("wk_in", [D, D], F32, kind="ExternalInput").ap()
    wv_in = nc.dram_tensor("wv_in", [D, D], F32, kind="ExternalInput").ap()
    w1_in = nc.dram_tensor("w1_in", [HID, D], F32, kind="ExternalInput").ap()
    w2_in = nc.dram_tensor("w2_in", [D, HID], F32, kind="ExternalInput").ap()
    cwm_in = nc.dram_tensor("cwm_in", [128, 32], F32, kind="ExternalInput").ap()
    cw0_in = nc.dram_tensor("cw0_in", [128, 32], F32, kind="ExternalInput").ap()
    cwp_in = nc.dram_tensor("cwp_in", [128, 32], F32, kind="ExternalInput").ap()
    cb_in = nc.dram_tensor("cb_in", [128, 32], F32, kind="ExternalInput").ap()
    out_o = nc.dram_tensor("out_o", [NSEQ, T, D], F32, kind="ExternalOutput").ap()

    xbf = nc.dram_tensor("xbf", [NSEQ, T, D], BF16, kind=K).ap()
    wqb = nc.dram_tensor("wqb", [D, D], BF16, kind=K).ap()
    wkb = nc.dram_tensor("wkb", [D, D], BF16, kind=K).ap()
    wvb = nc.dram_tensor("wvb", [D, D], BF16, kind=K).ap()
    wq1s = nc.dram_tensor("wq1s", [HID, D], BF16, kind=K).ap()
    wq2s = nc.dram_tensor("wq2s", [D, HID], BF16, kind=K).ap()
    ots = nc.dram_tensor("ots", [D, T], BF16, kind=K).ap()
    x2s = nc.dram_tensor("x2s", [T, D], F32, kind=K).ap()
    k1s = nc.dram_tensor("k1s", [T, D], BF16, kind=K).ap()
    xq2s = nc.dram_tensor("xq2s", [HID, T], BF16, kind=K).ap()

    gamq_np, msk_np, gamk_np = build_consts()
    i_gamq = nc.inline_tensor(gamq_np, "i_gamq")
    i_msk = nc.inline_tensor(msk_np, "i_msk")
    i_gamk = nc.inline_tensor(gamk_np, "i_gamk")
    i_identb = nc.inline_tensor(np.eye(128, dtype=ml_dtypes.bfloat16), "i_identb")
    i_identf = nc.inline_tensor(np.eye(128, dtype=np.float32), "i_identf")
    i_ones = nc.inline_tensor(np.ones((1, 128), dtype=np.float32), "i_ones")
    i_onescol = nc.inline_tensor(np.ones((128, 1), dtype=np.float32), "i_onescol")

    ctx0 = ExitStack()

    def sb(name, shape, dt, c=ctx0):
        return c.enter_context(nc.sbuf_tensor(name, shape, dt))

    def pst(name, shape, dt, c):
        return c.enter_context(nc.psum_tensor(name, shape, dt))

    c_gamq = sb("c_gamq", [128, L], F32)
    c_msk = sb("c_msk", [128, 4, L], F32)
    c_gamk = sb("c_gamk", [128, 4], F32)
    c_idb = sb("c_idb", [128, 128], BF16)
    c_idf = sb("c_idf", [128, 128], F32)
    c_ones = sb("c_ones", [1, 128], F32)
    c_onescol = sb("c_onescol", [128, 1], F32)
    c_cwm = sb("c_cwm", [128, 32], F32)
    c_cw0 = sb("c_cw0", [128, 32], F32)
    c_cwp = sb("c_cwp", [128, 32], F32)
    c_cb = sb("c_cb", [128, 32], F32)
    g_bsw1 = sb("g_bsw1", [128, 1], F32)
    g_bsw2 = sb("g_bsw2", [128, 1], F32)
    g_mw1 = sb("g_mw1", [128, 1], F32)
    g_mw2 = sb("g_mw2", [128, 1], F32)
    M2all = sb("M2all", [128, 32], F32)

    sLD = Ctr(nc, ctx0, "sLD")
    sST = Ctr(nc, ctx0, "sST")
    sCG = Ctr(nc, ctx0, "sCG")
    sT = Ctr(nc, ctx0, "sT")
    sV = Ctr(nc, ctx0, "sV")
    sS = Ctr(nc, ctx0, "sS")
    sG = Ctr(nc, ctx0, "sG")

    tens, vec, scal, sync, gps = nc.tensor, nc.vector, nc.scalar, nc.sync, nc.gpsimd

    for src, dst in [(i_gamq.ap(), c_gamq[:]), (i_msk.ap(), c_msk[:]),
                     (i_gamk.ap(), c_gamk[:]), (i_identb.ap(), c_idb[:]),
                     (i_identf.ap(), c_idf[:]), (i_ones.ap(), c_ones[:]),
                     (i_onescol.ap(), c_onescol[:]),
                     (cwm_in, c_cwm[:]), (cw0_in, c_cw0[:]),
                     (cwp_in, c_cwp[:]), (cb_in, c_cb[:])]:
        sLD.inc(sync.dma_start(out=dst, in_=src), 16)
    for s in range(NSEQ):
        sCG.inc(gps.dma_start(out=xbf[s], in_=x_in[s]), 16)
    for src, dst in [(wq_in, wqb), (wk_in, wkb), (wv_in, wvb)]:
        sCG.inc(gps.dma_start(out=dst, in_=src), 16)

    ctxP = ExitStack()
    wtile = [sb(f"p_wt{i}", [128, HID], F32, ctxP) for i in range(2)]
    wqout = [sb(f"p_wq{i}", [128, HID], BF16, ctxP) for i in range(2)]
    trash = sb("p_trash", [128, HID], F32, ctxP)
    SAB = sb("p_sab", [128, 32], F32, ctxP)
    redv = sb("p_red", [128, 1], F32, ctxP)
    sc1 = sb("p_sc1", [1, 8], F32, ctxP)
    pR = pst("p_pR", [1, 1], F32, ctxP)
    pB = pst("p_pB", [128, 2], F32, ctxP)

    def quant_weights(w_src, wq_dst, ntile, tilecols, nelem, b_sw, b_mw):
        sST.wait(scal)
        nc.all_engine_barrier()
        absdone = {}
        ldev = {}
        for i in range(ntile):
            if i >= 2:
                sS.wait(sync, absdone[i - 2])
            sLD.inc(sync.dma_start(out=wtile[i % 2][:, 0:tilecols],
                                   in_=w_src[128 * i:128 * (i + 1), :]), 16)
            ldev[i] = sLD.v
            sLD.wait(scal, ldev[i])
            absdone[i] = sS.inc(scal.activation(out=trash[:, 0:tilecols],
                                                in_=wtile[i % 2][:, 0:tilecols],
                                                func=AF.Abs, accum_out=SAB[:, i:i + 1]))
        sS.wait(vec)
        vec.reduce_sum(out=redv[:], in_=SAB[:, 0:ntile], axis=AX.X)
        vec.drain()
        ev = sV.inc(vec.engine_nop())
        sV.wait(tens, ev)
        tev = sT.inc(tens.matmul(out=pR[:], lhsT=redv[:], rhs=c_onescol[:], start=True, stop=True))
        sT.wait(scal, tev)
        sev = sS.inc(scal.copy(out=sc1[:, 0:1], in_=pR[:]))
        sS.wait(vec, sev)
        vec.tensor_scalar(out=sc1[:, 1:2], in0=sc1[:, 0:1], scalar1=1.0 / nelem,
                          scalar2=1e-5, op0=ALU.mult, op1=ALU.max)
        vec.drain()
        vec.tensor_scalar(out=sc1[:, 2:3], in0=sc1[:, 1:2], scalar1=1.0 / 127.0, scalar2=None,
                          op0=ALU.mult)
        vec.drain()
        ev = sV.inc(vec.engine_nop())
        sV.wait(tens, ev)
        tev = sT.inc(tens.matmul(out=pB[:, 0:2], lhsT=c_ones[:], rhs=sc1[0:1, 1:3],
                                 start=True, stop=True))
        sT.wait(scal, tev)
        scal.copy(out=b_sw[:], in_=pB[:, 0:1])
        sev = sS.inc(scal.copy(out=b_mw[:], in_=pB[:, 1:2]))
        sS.wait(vec, sev)
        vec.reciprocal(out=redv[:], in_=b_sw[:])
        vec.drain()
        vec.tensor_tensor(out=SAB[:, 0:1], in0=b_sw[:], in1=redv[:], op=ALU.mult)
        vec.drain()
        vec.tensor_scalar(out=SAB[:, 0:1], in0=SAB[:, 0:1], scalar1=-1.0, scalar2=2.0,
                          op0=ALU.mult, op1=ALU.add)
        vec.drain()
        nev = sV.inc(vec.tensor_tensor(out=b_sw[:], in0=redv[:], in1=SAB[:, 0:1], op=ALU.mult))
        sV.wait(scal, nev)
        sev = sS.inc(scal.copy(out=trash[0:1, 0:1], in_=trash[0:1, 0:1]))
        qev = {}
        dmev = {}
        for i in range(ntile):
            if i >= 2:
                sV.wait(sync, qev[i - 2])
            sLD.inc(sync.dma_start(out=wtile[i % 2][:, 0:tilecols],
                                   in_=w_src[128 * i:128 * (i + 1), :]), 16)
            ld2 = sLD.v
            sLD.wait(vec, ld2)
            if i == 0:
                sS.wait(vec, sev)
            if i >= 2:
                sST.wait(vec, dmev[i - 2])
            vec.tensor_scalar(out=trash[:, 0:tilecols], in0=wtile[i % 2][:, 0:tilecols],
                              scalar1=b_sw[:, 0:1], scalar2=MAGIC, op0=ALU.mult, op1=ALU.add)
            vec.tensor_scalar(out=trash[:, 0:tilecols], in0=trash[:, 0:tilecols],
                              scalar1=MAGIC, scalar2=1.0, op0=ALU.subtract, op1=ALU.min)
            qev[i] = sV.inc(vec.tensor_scalar(out=wqout[i % 2][:, 0:tilecols], in0=trash[:, 0:tilecols],
                                              scalar1=-1.0, scalar2=None, op0=ALU.max))
            sV.wait(scal, qev[i])
            dmev[i] = sST.inc(scal.dma_start(out=wq_dst[128 * i:128 * (i + 1), :],
                                             in_=wqout[i % 2][:, 0:tilecols]), 16)
        return dmev[ntile - 1]

    if "P0" in PHASES:
        quant_weights(w1_in, wq1s, HID // 128, D, float(HID) * D, g_bsw1, g_mw1)
        quant_weights(w2_in, wq2s, D // 128, HID, float(HID) * D, g_bsw2, g_mw2)
        if DBG:
            dbgo = nc.dram_tensor("dbgo", [128, 4], F32, kind="ExternalOutput").ap()
            for col, src_t in enumerate([g_bsw1, g_mw1, g_bsw2, g_mw2]):
                sST.inc(scal.dma_start(out=dbgo[:, col:col + 1], in_=src_t[:], allow_slow_non_contiguous=True), 16)
            dbgo2 = nc.dram_tensor("dbgo2", [128, 34], F32, kind="ExternalOutput").ap()
            sST.inc(scal.dma_start(out=dbgo2[:, 0:32], in_=SAB[:, 0:32]), 16)
            sST.inc(scal.dma_start(out=dbgo2[:, 32:33], in_=redv[:], allow_slow_non_contiguous=True), 16)
            sST.inc(scal.dma_start(out=dbgo2[:, 33:34], in_=g_bsw2[:], allow_slow_non_contiguous=True), 16)
    sST.wait(scal)
    sLD.wait(sync)
    ctxP.close()
    nc.all_engine_barrier()

    DUMP_B1 = None
    DUMP_B2 = None
    if DBG:
        DUMP_B1 = nc.dram_tensor("dbg_b1", [128, 12], F32, kind="ExternalOutput").ap()
        DUMP_B2 = dict(
            xq1t=nc.dram_tensor("dbg_xq1t", [128, 8, 288], BF16, kind="ExternalOutput").ap(),
            h1=nc.dram_tensor("dbg_h1", [128, 32, 258], BF16, kind="ExternalOutput").ap(),
            cv=nc.dram_tensor("dbg_cv", [128, 32, 256], BF16, kind="ExternalOutput").ap(),
            s2b=nc.dram_tensor("dbg_s2b", [128, 256], F32, kind="ExternalOutput").ap(),
            macc=nc.dram_tensor("dbg_macc", [128, 256], F32, kind="ExternalOutput").ap(),
        )
    DBGA = None
    if DBG and globals().get("DUMP_A"):
        DBGA = dict(
            qh=nc.dram_tensor("dbg_qh", [128, 8, L], BF16, kind="ExternalOutput").ap(),
            kt=nc.dram_tensor("dbg_kt", [128, 8, L], BF16, kind="ExternalOutput").ap(),
            vt=nc.dram_tensor("dbg_vt", [128, 4, D], BF16, kind="ExternalOutput").ap(),
            at=nc.dram_tensor("dbg_at", [128, 4, L], BF16, kind="ExternalOutput").ap(),
            xt=nc.dram_tensor("dbg_xt", [128, 8, L], BF16, kind="ExternalOutput").ap(),
            ot0=nc.dram_tensor("dbg_ot0", [128, 8, L], BF16, kind="ExternalOutput").ap(),
            s1=nc.dram_tensor("dbg_s1", [128, 8, D], BF16, kind="ExternalOutput").ap(),
        )
    for s in range(NSEQ):
        if "A" in PHASES:
            phase_A(nc, s, locals())
            nc.all_engine_barrier()
        if "B1" in PHASES:
            phase_B1(nc, s, locals())
            nc.all_engine_barrier()
        if "B2a" in PHASES:
            phase_B2a(nc, s, locals())
            nc.all_engine_barrier()
        if "B2b" in PHASES:
            phase_B2b(nc, s, locals())
            nc.all_engine_barrier()

    sST.wait(sync)
    sST.wait(scal)
    ctx0.close()
    return nc


def phase_A(nc, s, env):
    g = env
    sb_ = lambda n, sh, dt, c: c.enter_context(nc.sbuf_tensor(n + f"_s{s}", sh, dt))
    ps_ = lambda n, sh, dt, c: c.enter_context(nc.psum_tensor(n + f"_s{s}", sh, dt))
    tens, vec, scal, sync = nc.tensor, nc.vector, nc.scalar, nc.sync
    sLD, sST, sCG, sT, sV, sS = g["sLD"], g["sST"], g["sCG"], g["sT"], g["sV"], g["sS"]
    c_gamq, c_msk, c_gamk, c_idb = g["c_gamq"], g["c_msk"], g["c_gamk"], g["c_idb"]
    xbf, ots, wqb, wkb, wvb = g["xbf"], g["ots"], g["wqb"], g["wkb"], g["wvb"]

    ctx = ExitStack()
    aWQ = sb_("aWQ", [128, 8, D], BF16, ctx)
    aWK = sb_("aWK", [128, 8, D], BF16, ctx)
    aWV = sb_("aWV", [128, 8, D], BF16, ctx)
    aS = sb_("aS", [128, 8, D], BF16, ctx)
    aXT = [sb_(f"aXT{i}", [128, 8, L], BF16, ctx) for i in range(2)]
    aQH = sb_("aQH", [128, 8, L], BF16, ctx)
    aKT = sb_("aKT", [128, 8, L], BF16, ctx)
    aVT = sb_("aVT", [128, 4, D], BF16, ctx)
    aKTM = sb_("aKTM", [128, 4, D], BF16, ctx)
    aAT = sb_("aAT", [128, 4, L], BF16, ctx)
    aOT = [sb_(f"aOT{i}", [128, 8, L], BF16, ctx) for i in range(2)]
    pQV = [ps_(f"pQV{i}", [128, L], F32, ctx) for i in range(2)]
    pAT = ps_("pAT", [128, L], F32, ctx)
    pO = [ps_(f"pO{i}", [128, L], F32, ctx) for i in range(2)]
    pDS = [ps_(f"pDS{i}", [128, 512], F32, ctx) for i in range(2)]
    pTR = ps_("pTR", [128, 8, 128], BF16, ctx)

    sCG.wait(sync)
    for src, dst in [(wqb, aWQ), (wkb, aWK), (wvb, aWV)]:
        sLD.inc(sync.dma_start(out=dst[:], in_=src.rearrange("(dc p) c -> p dc c", p=128)), 16)
    ld_w = sLD.v
    ev_memset = sV.inc(vec.memset(aS[:], 0.0))

    ld_x = {}
    qkv_done = {}
    o_done = {}
    ds_done = {}
    attr_done = {}
    ot_dma = {}
    bank_guard = {}

    def loads(c):
        if c >= 2:
            sT.wait(sync, qkv_done[c - 2])
        for f in range(8):
            sLD.inc(sync.dma_start(out=aXT[c % 2][:, f, :],
                                   in_=xbf[s, L * c:L * (c + 1), 128 * f:128 * (f + 1)],
                                   transpose=True), 16)
        ld_x[c] = sLD.v

    loads(0)
    loads(1)

    for c in range(NCH):
        par = c % 2
        sLD.wait(tens, ld_x[c])
        if c == 0:
            sLD.wait(tens, ld_w)
            sV.wait(tens, ev_memset)
        grp = 0
        if c > 0:
            sT.wait(vec, o_done[c - 1])
            sT.wait(scal, attr_done[c - 1])
            sT.wait(scal, ds_done[c - 1])
        for cc in range(8):
            b = pQV[grp % 2]
            if ("qv", grp % 2) in bank_guard:
                ctr, val = bank_guard[("qv", grp % 2)]
                ctr.wait(tens, val)
            for dc in range(8):
                mi = tens.matmul(out=b[:], lhsT=aWQ[:, dc, 128 * cc:128 * (cc + 1)],
                            rhs=aXT[par][:, dc, :], start=(dc == 0), stop=(dc == 7))
            tev = sT.inc(mi)
            sT.wait(vec, tev)
            vev = sV.inc(vec.tensor_tensor(out=aQH[:, cc, :], in0=b[:], in1=c_gamq[:], op=ALU.mult))
            bank_guard[("qv", grp % 2)] = (sV, vev)
            grp += 1
        for cc in range(8):
            b = pQV[grp % 2]
            ctr, val = bank_guard[("qv", grp % 2)]
            ctr.wait(tens, val)
            for dc in range(8):
                mi = tens.matmul(out=b[:], lhsT=aWK[:, dc, 128 * cc:128 * (cc + 1)],
                            rhs=aXT[par][:, dc, :], start=(dc == 0), stop=(dc == 7))
            tev = sT.inc(mi)
            sT.wait(scal, tev)
            sev = sS.inc(scal.copy(out=aKT[:, cc, :], in_=b[:]))
            bank_guard[("qv", grp % 2)] = (sS, sev)
            grp += 1
        for ts_ in range(4):
            for hf in range(2):
                b = pQV[grp % 2]
                ctr, val = bank_guard[("qv", grp % 2)]
                ctr.wait(tens, val)
                for dc in range(8):
                    mi = tens.matmul(out=b[:], lhsT=aXT[par][:, dc, 128 * ts_:128 * (ts_ + 1)],
                                rhs=aWV[:, dc, 512 * hf:512 * (hf + 1)],
                                start=(dc == 0), stop=(dc == 7))
                tev = sT.inc(mi)
                sT.wait(scal, tev)
                sev = sS.inc(scal.copy(out=aVT[:, ts_, 512 * hf:512 * (hf + 1)], in_=b[:]))
                bank_guard[("qv", grp % 2)] = (sS, sev)
                grp += 1
        qkv_done[c] = sT.v
        kt_all = sS.v
        qh_all = sV.v

        _AS = globals().get('A_STAGES', 6)
        if _AS >= 2:
            sS.wait(tens, kt_all)
            if c > 0:
                sT.wait(vec, ds_done[c - 1])
            for i in range(4):
                if ("tr",) in bank_guard:
                    ctr, val = bank_guard[("tr",)]
                    ctr.wait(tens, val)
                for cc in range(8):
                    mi = tens.transpose(out=pTR[:, cc, :], in_=aKT[:, cc, 128 * i:128 * (i + 1)],
                                   identity=c_idb[:])
                tev = sT.inc(mi)
                sT.wait(vec, tev)
                vev = sV.inc(vec.tensor_scalar(out=aKTM[:, i, :], in0=pTR[:].rearrange("p a b -> p (a b)"),
                                               scalar1=c_gamk[:, i:i + 1], scalar2=None, op0=ALU.mult))
                bank_guard[("tr",)] = (sV, vev)
        ktm_all = sV.v
        if _AS >= 3:
            sV.wait(tens, qh_all)
            if c > 0:
                sT.wait(vec, o_done[c - 1])
            for i in range(4):
                if ("at",) in bank_guard:
                    ctr, val = bank_guard[("at",)]
                    ctr.wait(tens, val)
                n0 = 128 * i
                for cc in range(8):
                    mi = tens.matmul(out=pAT[:, n0:L], lhsT=aKT[:, cc, 128 * i:128 * (i + 1)],
                                rhs=aQH[:, cc, n0:L], start=(cc == 0), stop=(cc == 7))
                tev = sT.inc(mi)
                sT.wait(vec, tev)
                vev = sV.inc(vec.tensor_tensor(out=aAT[:, i, n0:L], in0=pAT[:, n0:L],
                                               in1=c_msk[:, i, n0:L], op=ALU.mult))
                bank_guard[("at",)] = (sV, vev)
            at_all = sV.v
        at_all = sV.v
        attr_done[c] = sT.v
        if _AS >= 4:
            sV.wait(tens, at_all)
            if c > 0:
                sST.wait(scal, ot_dma.get(c - 2, 0))
            for cc in range(8):
                b = pO[cc % 2]
                if ("o", cc % 2) in bank_guard:
                    ctr, val = bank_guard[("o", cc % 2)]
                    ctr.wait(tens, val)
                for i in range(4):
                    n0 = 128 * i
                    mi = tens.matmul(out=b[:, n0:L], lhsT=aVT[:, i, 128 * cc:128 * (cc + 1)],
                                rhs=aAT[:, i, n0:L], start=(i == 0), stop=False)
                for ck in range(8):
                    mi = tens.matmul(out=b[:], lhsT=aS[:, ck, 128 * cc:128 * (cc + 1)],
                                rhs=aQH[:, ck, :], start=False, stop=(ck == 7))
                tev = sT.inc(mi)
                sT.wait(scal, tev)
                sev = sS.inc(scal.copy(out=aOT[par][:, cc, :], in_=b[:]))
                bank_guard[("o", cc % 2)] = (sS, sev)
            o_done[c] = sT.v
            o_evac = sS.v
        o_done[c] = sT.v
        if _AS >= 5:
            sV.wait(tens, ktm_all)
            for ck in range(8):
                for hf in range(2):
                    if ("ds", hf) in bank_guard:
                        ctr, val = bank_guard[("ds", hf)]
                        ctr.wait(tens, val)
                    b = pDS[hf]
                    for i in range(4):
                        mi = tens.matmul(out=b[:], lhsT=aKTM[:, i, 128 * ck:128 * (ck + 1)],
                                    rhs=aVT[:, i, 512 * hf:512 * (hf + 1)],
                                    start=(i == 0), stop=(i == 3))
                    tev = sT.inc(mi)
                    sT.wait(vec, tev)
                    vev = sV.inc(vec.scalar_tensor_tensor(
                        out=aS[:, ck, 512 * hf:512 * (hf + 1)],
                        in0=aS[:, ck, 512 * hf:512 * (hf + 1)],
                        scalar=float(_gp(512)), in1=b[:], op0=ALU.mult, op1=ALU.add))
                    bank_guard[("ds", hf)] = (sV, vev)
        ds_done[c] = sT.v
        if _AS >= 6:
            for gg in range(8):
                ot_dma[c] = sST.inc(scal.dma_start(
                    out=ots[128 * gg:128 * (gg + 1), L * c:L * (c + 1)],
                    in_=aOT[par][:, gg, :]), 16)
        else:
            ot_dma[c] = 0
        if g.get("DBGA") is not None and s == 0 and c == 0:
            dbga = g["DBGA"]
            sV.wait(scal)
            sS.wait(scal)
            sST.inc(scal.dma_start(out=dbga["qh"], in_=aQH[:]), 16)
            sST.inc(scal.dma_start(out=dbga["kt"], in_=aKT[:]), 16)
            sST.inc(scal.dma_start(out=dbga["vt"], in_=aVT[:]), 16)
            sST.inc(scal.dma_start(out=dbga["at"], in_=aAT[:]), 16)
            sST.inc(scal.dma_start(out=dbga["xt"], in_=aXT[par][:]), 16)
            sST.inc(scal.dma_start(out=dbga["ot0"], in_=aOT[par][:]), 16)
            sST.inc(scal.dma_start(out=dbga["s1"], in_=aS[:]), 16)
            sST.wait(scal)
            sST.wait(vec)
            sST.wait(sync)
            sST.wait(tens)

        if c + 2 < NCH:
            loads(c + 2)

    sST.wait(scal)
    sLD.wait(sync)
    ctx.close()


def phase_B1(nc, s, env):
    g = env
    sb_ = lambda n, sh, dt, c: c.enter_context(nc.sbuf_tensor(n + f"_b1s{s}", sh, dt))
    tens, vec, scal, sync = nc.tensor, nc.vector, nc.scalar, nc.sync
    sLD, sST, sT, sV, sS = g["sLD"], g["sST"], g["sT"], g["sV"], g["sS"]
    x_in, ots, x2s, k1s, g_mw1 = g["x_in"], g["ots"], g["x2s"], g["k1s"], g["g_mw1"]

    ctx = ExitStack()
    xt = [sb_(f"xt{i}", [128, D], F32, ctx) for i in range(2)]
    ot = [sb_(f"ot{i}", [128, D], BF16, ctx) for i in range(2)]
    rW = [sb_(f"rW{i}", [128, D], F32, ctx) for i in range(2)]
    x2t = [sb_(f"x2t{i}", [128, D], F32, ctx) for i in range(2)]
    k1t = [sb_(f"k1t{i}", [128, D], BF16, ctx) for i in range(2)]
    tmpD = sb_("tmpD", [128, D], F32, ctx)
    trash = sb_("trashb1", [128, D], F32, ctx)
    st_ = sb_("stats", [128, 12], F32, ctx)
    ot_flat = ots.flatten()

    ld = {}
    ev_r = {}
    ev_x2dma = {}
    ev_k1dma = {}
    ev_sq = {}
    ev_done = {}

    def loads(j):
        if j >= 2:
            sV.wait(sync, ev_done[j - 2])
        sLD.inc(sync.dma_start(out=xt[j % 2][:], in_=x_in[s, 128 * j:128 * (j + 1), :]), 16)
        sLD.inc(sync.dma_start(out=ot[j % 2][:],
                               in_=ot_flat[128 * D * j:128 * D * (j + 1)].rearrange("(p t) -> p t", p=128)), 16)
        ld[j] = sLD.v

    loads(0)
    loads(1)
    for j in range(NT):
        par = j % 2
        sLD.wait(vec, ld[j])
        if j >= 2:
            sS.wait(vec, ev_sq[j - 2])
            sST.wait(vec, ev_x2dma[j - 2])
            sST.wait(vec, ev_k1dma[j - 2])
        ev_r[j] = sV.inc(vec.tensor_tensor(out=rW[par][:], in0=xt[par][:], in1=ot[par][:], op=ALU.add))
        vec.reduce_sum(out=st_[:, 0:1], in_=rW[par][:], axis=AX.X)
        vec.drain()
        sV.wait(scal, ev_r[j])
        ev_sq[j] = sS.inc(scal.activation(out=trash[:], in_=rW[par][:], func=AF.Square, accum_out=st_[:, 1:2]))
        sS.wait(vec, ev_sq[j])
        vec.tensor_scalar(out=st_[:, 2:3], in0=st_[:, 0:1], scalar1=1.0 / D, scalar2=None, op0=ALU.mult)
        vec.tensor_scalar(out=st_[:, 3:4], in0=st_[:, 1:2], scalar1=1.0 / D, scalar2=None, op0=ALU.mult)
        vec.drain()
        vec.scalar_tensor_tensor(out=st_[:, 4:5], in0=st_[:, 2:3], scalar=st_[:, 2:3],
                                 in1=st_[:, 3:4], op0=ALU.mult, op1=ALU.subtract)
        vec.drain()
        vec.tensor_scalar(out=st_[:, 5:6], in0=st_[:, 4:5], scalar1=-1.0, scalar2=EPS,
                          op0=ALU.mult, op1=ALU.add)
        vec.drain()
        ev = sV.inc(vec.reciprocal(out=st_[:, 6:7], in_=st_[:, 5:6]))
        sV.wait(scal, ev)
        ev = sS.inc(scal.activation(out=st_[:, 7:8], in_=st_[:, 6:7], func=AF.Sqrt))
        sS.wait(vec, ev)
        vec.tensor_tensor(out=st_[:, 8:9], in0=st_[:, 2:3], in1=st_[:, 7:8], op=ALU.mult)
        vec.drain()
        vec.tensor_scalar(out=st_[:, 8:9], in0=st_[:, 8:9], scalar1=-1.0, scalar2=None, op0=ALU.mult)
        vec.drain()
        ev_x2 = sV.inc(vec.tensor_scalar(out=x2t[par][:], in0=rW[par][:], scalar1=st_[:, 7:8],
                                         scalar2=st_[:, 8:9], op0=ALU.mult, op1=ALU.add))
        sV.wait(scal, ev_x2)
        ev_x2dma[j] = sST.inc(scal.dma_start(out=x2s[128 * j:128 * (j + 1), :], in_=x2t[par][:]), 16)
        vec.reduce_max(out=st_[:, 9:10], in_=x2t[par][:], axis=AX.X, apply_absolute_value=True)
        vec.drain()
        vec.tensor_scalar(out=st_[:, 9:10], in0=st_[:, 9:10], scalar1=1e-5, scalar2=None, op0=ALU.max)
        vec.drain()
        vec.reciprocal(out=st_[:, 10:11], in_=st_[:, 9:10])
        vec.drain()
        vec.tensor_tensor(out=st_[:, 11:12], in0=st_[:, 9:10], in1=st_[:, 10:11], op=ALU.mult)
        vec.drain()
        vec.tensor_scalar(out=st_[:, 11:12], in0=st_[:, 11:12], scalar1=-1.0, scalar2=2.0,
                          op0=ALU.mult, op1=ALU.add)
        vec.drain()
        vec.tensor_tensor(out=st_[:, 10:11], in0=st_[:, 10:11], in1=st_[:, 11:12], op=ALU.mult)
        vec.drain()
        vec.tensor_scalar(out=st_[:, 11:12], in0=st_[:, 10:11], scalar1=127.0, scalar2=None, op0=ALU.mult)
        vec.tensor_tensor(out=st_[:, 9:10], in0=st_[:, 9:10], in1=g_mw1[:], op=ALU.mult)
        vec.drain()
        vec.tensor_scalar(out=tmpD[:], in0=x2t[par][:], scalar1=st_[:, 11:12], scalar2=MAGIC,
                          op0=ALU.mult, op1=ALU.add)
        ev_k1 = sV.inc(vec.tensor_scalar(out=k1t[par][:], in0=tmpD[:], scalar1=MAGIC,
                                         scalar2=st_[:, 9:10], op0=ALU.subtract, op1=ALU.mult))
        ev_done[j] = ev_k1
        sV.wait(scal, ev_k1)
        ev_k1dma[j] = sST.inc(scal.dma_start(out=k1s[128 * j:128 * (j + 1), :], in_=k1t[par][:]), 16)
        if j + 2 < NT:
            loads(j + 2)

    if g.get("DUMP_B1") is not None:
        sV.wait(scal)
        sST.inc(scal.dma_start(out=g["DUMP_B1"], in_=st_[:]), 16)
    sST.wait(scal)
    sLD.wait(sync)
    ctx.close()


def phase_B2a(nc, s, env):
    g = env
    sb_ = lambda n, sh, dt, c: c.enter_context(nc.sbuf_tensor(n + f"_b2as{s}", sh, dt))
    ps_ = lambda n, sh, dt, c: c.enter_context(nc.psum_tensor(n + f"_b2as{s}", sh, dt))
    tens, vec, scal, sync, gps = nc.tensor, nc.vector, nc.scalar, nc.sync, nc.gpsimd
    sLD, sST, sT, sV, sS, sG = g["sLD"], g["sST"], g["sT"], g["sV"], g["sS"], g["sG"]
    k1s, xq2s, wq1s = g["k1s"], g["xq2s"], g["wq1s"]
    M2all = g["M2all"]
    c_cwm, c_cw0, c_cwp, c_cb = g["c_cwm"], g["c_cw0"], g["c_cwp"], g["c_cb"]
    c_idf, c_ones = g["c_idf"], g["c_ones"]

    ctx = ExitStack()
    WQ1T = sb_("WQ1T", [128, 8, HID], BF16, ctx)
    XQ1T = [sb_(f"XQ1T{i}", [128, 8, 288], BF16, ctx) for i in range(2)]
    H1 = [sb_(f"H1_{i}", [128, 32, 258], BF16, ctx) for i in range(2)]
    CV = sb_("CV", [128, 32, 256], BF16, ctx)
    XQ2 = [sb_(f"XQ2_{i}", [128, 32, SLAB], BF16, ctx) for i in range(2)]
    TMP = [sb_(f"TMP{i}", [128, SLAB], F32, ctx) for i in range(2)]
    macc = sb_("macc", [128, SLAB], F32, ctx)
    S2B = sb_("S2B", [128, SLAB], F32, ctx)
    s2t = sb_("s2t", [128, 2], F32, ctx)
    s2w = sb_("s2w", [128, 2], F32, ctx)
    s2row = sb_("s2row", [1, 256], F32, ctx)
    pH = [ps_(f"pH{i}", [128, 258], F32, ctx) for i in range(2)]
    pM = ps_("pM", [128, 2, 128], F32, ctx)
    pM2 = ps_("pM2", [1, 256], F32, ctx)
    pS2 = ps_("pS2", [128, SLAB], F32, ctx)

    for dc in range(8):
        sLD.inc(sync.dma_start(out=WQ1T[:, dc, :], in_=wq1s[:, 128 * dc:128 * (dc + 1)],
                               transpose=True), 16)
    ld_w = sLD.v

    ld = {}
    mm1_done = {}
    halo = {}
    qdone = {}
    xq2dma = {}
    silu_ev = {}
    bank = {}

    def loads(sl):
        if sl >= 2:
            sT.wait(sync, mm1_done[sl - 2])
        if sl >= 1:
            sV.wait(sync, halo[sl - 1])
        t0 = SLAB * sl
        rows = 272 if sl < NSLAB - 1 else 256
        for dc in range(8):
            sLD.inc(sync.dma_start(out=XQ1T[sl % 2][:, dc, 16:16 + rows],
                                   in_=k1s[t0:t0 + rows, 128 * dc:128 * (dc + 1)],
                                   transpose=True), 16)
        ld[sl] = sLD.v

    loads(0)
    for sl in range(NSLAB):
        par = sl % 2
        sLD.wait(vec, ld[sl])
        if sl == 0:
            vec.memset(XQ1T[par][:, :, 15:16], 0.0)
        else:
            vec.tensor_copy(out=XQ1T[par][:, :, 15:16], in_=XQ1T[1 - par][:, :, 271:272])
        if sl == NSLAB - 1:
            vec.memset(XQ1T[par][:, :, 272:273], 0.0)
        vec.drain()
        halo[sl] = sV.inc(vec.engine_nop())
        if sl + 1 < NSLAB:
            loads(sl + 1)
        sV.wait(tens, halo[sl])
        if sl == 0:
            sLD.wait(tens, ld_w)
        ev_evac = {}
        for hc in range(32):
            if ("h", hc % 2) in bank:
                ctr, val = bank[("h", hc % 2)]
                ctr.wait(tens, val)
            b = pH[hc % 2]
            for dc in range(8):
                mi = tens.matmul(out=b[:], lhsT=WQ1T[:, dc, 128 * hc:128 * (hc + 1)],
                            rhs=XQ1T[par][:, dc, 15:273], start=(dc == 0), stop=(dc == 7))
            tev = sT.inc(mi)
            sT.wait(scal, tev)
            if hc == 0 and sl >= 2:
                sV.wait(scal, qdone[sl - 2])
            sev = sS.inc(scal.copy(out=H1[par][:, hc, :], in_=b[:]))
            ev_evac[hc] = sev
            bank[("h", hc % 2)] = (sS, sev)
        mm1_done[sl] = sT.v
        for hc in range(32):
            sS.wait(vec, ev_evac[hc])
            if hc == 0 and sl >= 1:
                sS.wait(vec, silu_ev[sl - 1])
            vec.tensor_scalar(out=CV[:, hc, :], in0=H1[par][:, hc, 1:257],
                              scalar1=c_cw0[:, hc:hc + 1], scalar2=c_cb[:, hc:hc + 1],
                              op0=ALU.mult, op1=ALU.add)
            vec.scalar_tensor_tensor(out=CV[:, hc, :], in0=H1[par][:, hc, 0:256],
                                     scalar=c_cwm[:, hc:hc + 1], in1=CV[:, hc, :],
                                     op0=ALU.mult, op1=ALU.add)
            vec.scalar_tensor_tensor(out=CV[:, hc, :], in0=H1[par][:, hc, 2:258],
                                     scalar=c_cwp[:, hc:hc + 1], in1=CV[:, hc, :],
                                     op0=ALU.mult, op1=ALU.add)
        conv_ev = sV.inc(vec.tensor_copy(out=CV[:, 0:1, 0:1], in_=CV[:, 0:1, 0:1]))
        sV.wait(scal, conv_ev)
        silu_ev[sl] = sS.inc(scal.activation(out=H1[par][:, :, 0:256],
                                             in_=CV[:].rearrange("p a b -> p (a b)"), func=AF.Silu))
        sS.wait(vec, silu_ev[sl])
        mev = sV.inc(vec.reduce_max(out=macc[:], in_=H1[par][:, :, 0:256].rearrange("p hc t -> p t hc"),
                                    axis=AX.X, apply_absolute_value=True))
        sV.wait(tens, mev)
        if ("m",) in bank:
            ctr, val = bank[("m",)]
            ctr.wait(tens, val)
        mi = tens.transpose(out=pM[:, 0, :], in_=macc[:, 0:128], identity=c_idf[:])
        mi = tens.transpose(out=pM[:, 1, :], in_=macc[:, 128:256], identity=c_idf[:])
        tev = sT.inc(mi)
        sT.wait(vec, tev)
        for jj in range(2):
            vec.reduce_max(out=s2t[:, jj:jj + 1], in_=pM[:, jj, :], axis=AX.X)
        vec.drain()
        vev = sV.inc(vec.tensor_scalar(out=s2t[:], in0=s2t[:], scalar1=1e-5, scalar2=None, op0=ALU.max))
        bank[("m",)] = (sV, vev)
        vec.drain()
        vec.tensor_copy(out=M2all[:, 2 * sl:2 * sl + 2], in_=s2t[:])
        vec.reciprocal(out=s2w[:], in_=s2t[:])
        vec.drain()
        vec.tensor_tensor(out=s2t[:], in0=s2t[:], in1=s2w[:], op=ALU.mult)
        vec.drain()
        vec.tensor_scalar(out=s2t[:], in0=s2t[:], scalar1=-1.0, scalar2=2.0, op0=ALU.mult, op1=ALU.add)
        vec.drain()
        vec.tensor_tensor(out=s2w[:], in0=s2w[:], in1=s2t[:], op=ALU.mult)
        vec.drain()
        vev = sV.inc(vec.tensor_scalar(out=s2w[:], in0=s2w[:], scalar1=127.0, scalar2=None, op0=ALU.mult))
        sV.wait(tens, vev)
        if ("m2",) in bank:
            ctr, val = bank[("m2",)]
            ctr.wait(tens, val)
        mi = tens.transpose(out=pM2[0:1, 0:128], in_=s2w[:, 0:1], identity=c_idf[:])
        mi = tens.transpose(out=pM2[0:1, 128:256], in_=s2w[:, 1:2], identity=c_idf[:])
        tev = sT.inc(mi)
        sT.wait(scal, tev)
        sev = sS.inc(scal.copy(out=s2row[:], in_=pM2[:]))
        bank[("m2",)] = (sS, sev)
        sS.wait(tens, sev)
        if ("s2b",) in bank:
            ctr, val = bank[("s2b",)]
            ctr.wait(tens, val)
        mi = tens.matmul(out=pS2[:], lhsT=c_ones[:], rhs=s2row[0:1, :], start=True, stop=True)
        tev = sT.inc(mi)
        sT.wait(scal, tev)
        s2b_ev = sS.inc(scal.copy(out=S2B[:], in_=pS2[:]))
        bank[("s2b",)] = (sS, s2b_ev)
        sS.wait(vec, s2b_ev)
        ev_tmp = {}
        ev_mag = {}
        for hc in range(32):
            if hc >= 2:
                sG.wait(vec, ev_mag[hc - 2])
            if hc == 0 and sl >= 2:
                sST.wait(gps, xq2dma[sl - 2])
            ev_tmp[hc] = sV.inc(vec.tensor_tensor(out=TMP[hc % 2][:], in0=H1[par][:, hc, 0:256],
                                                  in1=S2B[:], op=ALU.mult))
            sV.wait(gps, ev_tmp[hc])
            ev_mag[hc] = sG.inc(gps.tensor_scalar(out=XQ2[par][:, hc, :], in0=TMP[hc % 2][:],
                                                  scalar1=MAGIC, scalar2=MAGIC,
                                                  op0=ALU.add, op1=ALU.subtract))
        qdone[sl] = ev_tmp[31]
        sG.wait(scal, ev_mag[31])
        for gg in range(32):
            xq2dma[sl] = sST.inc(scal.dma_start(
                out=xq2s[128 * gg:128 * (gg + 1), SLAB * sl:SLAB * (sl + 1)],
                in_=XQ2[par][:, gg, :]), 16)
    if g.get("DUMP_B2") is not None and s == 1:
        d2 = g["DUMP_B2"]
        sV.wait(scal)
        sG.wait(scal)
        sST.inc(scal.dma_start(out=d2["xq1t"], in_=XQ1T[1][:]), 16)
        sST.inc(scal.dma_start(out=d2["h1"], in_=H1[1][:]), 16)
        sST.inc(scal.dma_start(out=d2["cv"], in_=CV[:]), 16)
        sST.inc(scal.dma_start(out=d2["s2b"], in_=S2B[:]), 16)
        sST.inc(scal.dma_start(out=d2["macc"], in_=macc[:]), 16)
    sST.wait(scal)
    sLD.wait(sync)
    ctx.close()


def phase_B2b(nc, s, env):
    g = env
    sb_ = lambda n, sh, dt, c: c.enter_context(nc.sbuf_tensor(n + f"_b2bs{s}", sh, dt))
    ps_ = lambda n, sh, dt, c: c.enter_context(nc.psum_tensor(n + f"_b2bs{s}", sh, dt))
    tens, vec, scal, sync = nc.tensor, nc.vector, nc.scalar, nc.sync
    sLD, sST, sT, sV, sS = g["sLD"], g["sST"], g["sT"], g["sV"], g["sS"]
    xq2s, wq2s, x2s, out_o, g_mw2, M2all = g["xq2s"], g["wq2s"], g["x2s"], g["out_o"], g["g_mw2"], g["M2all"]

    ctx = ExitStack()
    WQ2T = sb_("WQ2T", [128, 32, D], BF16, ctx)
    XQI = [sb_(f"XQI{i}", [128, 32, 128], BF16, ctx) for i in range(2)]
    x2t = [sb_(f"x2t{i}", [128, D], F32, ctx) for i in range(2)]
    rW = [sb_(f"rW{i}", [128, D], F32, ctx) for i in range(2)]
    outt = [sb_(f"outt{i}", [128, D], F32, ctx) for i in range(2)]
    trash = sb_("trashb2", [128, D], F32, ctx)
    st_ = sb_("stats2", [128, 12], F32, ctx)
    pO2 = [ps_(f"pO2_{i}", [128, D], F32, ctx) for i in range(2)]

    for hc in range(32):
        sLD.inc(sync.dma_start(out=WQ2T[:, hc, :], in_=wq2s[:, 128 * hc:128 * (hc + 1)],
                               transpose=True), 16)
    ld_w = sLD.v

    ld = {}
    ev_r2 = {}
    ev_outdma = {}
    ev_sq = {}
    mm_done = {}

    def loads(j):
        if j >= 2:
            sT.wait(sync, mm_done[j - 2])
            sV.wait(sync, ev_r2[j - 2])
        sLD.inc(sync.dma_start(out=XQI[j % 2][:],
                               in_=xq2s[:, 128 * j:128 * (j + 1)].rearrange("(g p) t -> p g t", p=128)), 16)
        sLD.inc(sync.dma_start(out=x2t[j % 2][:], in_=x2s[128 * j:128 * (j + 1), :]), 16)
        ld[j] = sLD.v

    loads(0)
    loads(1)
    for j in range(NT):
        par = j % 2
        sLD.wait(tens, ld[j])
        if j == 0:
            sLD.wait(tens, ld_w)
        if j >= 2:
            sV.wait(tens, ev_r2[j - 2])
        b = pO2[par]
        for hc in range(32):
            for hf in range(2):
                mi = tens.matmul(out=b[:, 512 * hf:512 * (hf + 1)], lhsT=XQI[par][:, hc, :],
                            rhs=WQ2T[:, hc, 512 * hf:512 * (hf + 1)],
                            start=(hc == 0), stop=(hc == 31))
        mm_done[j] = sT.inc(mi)
        vec.tensor_tensor(out=st_[:, 9:10], in0=M2all[:, j:j + 1], in1=g_mw2[:], op=ALU.mult)
        vec.drain()
        sT.wait(vec, mm_done[j])
        if j >= 2:
            sS.wait(vec, ev_sq[j - 2])
            sST.wait(vec, ev_outdma[j - 2])
        ev_r2[j] = sV.inc(vec.scalar_tensor_tensor(out=rW[par][:], in0=b[:],
                                                   scalar=st_[:, 9:10], in1=x2t[par][:],
                                                   op0=ALU.mult, op1=ALU.add))
        vec.reduce_sum(out=st_[:, 0:1], in_=rW[par][:], axis=AX.X)
        vec.drain()
        sV.wait(scal, ev_r2[j])
        ev_sq[j] = sS.inc(scal.activation(out=trash[:], in_=rW[par][:], func=AF.Square, accum_out=st_[:, 1:2]))
        sS.wait(vec, ev_sq[j])
        vec.tensor_scalar(out=st_[:, 2:3], in0=st_[:, 0:1], scalar1=1.0 / D, scalar2=None, op0=ALU.mult)
        vec.tensor_scalar(out=st_[:, 3:4], in0=st_[:, 1:2], scalar1=1.0 / D, scalar2=None, op0=ALU.mult)
        vec.drain()
        vec.scalar_tensor_tensor(out=st_[:, 4:5], in0=st_[:, 2:3], scalar=st_[:, 2:3],
                                 in1=st_[:, 3:4], op0=ALU.mult, op1=ALU.subtract)
        vec.drain()
        vec.tensor_scalar(out=st_[:, 5:6], in0=st_[:, 4:5], scalar1=-1.0, scalar2=EPS,
                          op0=ALU.mult, op1=ALU.add)
        vec.drain()
        ev = sV.inc(vec.reciprocal(out=st_[:, 6:7], in_=st_[:, 5:6]))
        sV.wait(scal, ev)
        ev = sS.inc(scal.activation(out=st_[:, 7:8], in_=st_[:, 6:7], func=AF.Sqrt))
        sS.wait(vec, ev)
        vec.tensor_tensor(out=st_[:, 8:9], in0=st_[:, 2:3], in1=st_[:, 7:8], op=ALU.mult)
        vec.drain()
        vec.tensor_scalar(out=st_[:, 8:9], in0=st_[:, 8:9], scalar1=-1.0, scalar2=None, op0=ALU.mult)
        vec.drain()
        ev_out = sV.inc(vec.tensor_scalar(out=outt[par][:], in0=rW[par][:], scalar1=st_[:, 7:8],
                                          scalar2=st_[:, 8:9], op0=ALU.mult, op1=ALU.add))
        sV.wait(scal, ev_out)
        ev_outdma[j] = sST.inc(scal.dma_start(out=out_o[s, 128 * j:128 * (j + 1), :],
                                              in_=outt[par][:]), 16)
        if j + 2 < NT:
            loads(j + 2)

    sST.wait(scal)
    sLD.wait(sync)
    ctx.close()


_NC_CACHE = None


def _get_nc():
    global _NC_CACHE
    if _NC_CACHE is None:
        _NC_CACHE = build_nc()
    return _NC_CACHE


def kernel(L0_attn_X, L0_attn_att_mask, L0_attn_S_n, L0_attn_W_Q, L0_attn_W_K,
           L0_attn_W_V, L0_ffn_w1, L0_ffn_conv_w, L0_ffn_conv_b, L0_ffn_w2):
    X = np.ascontiguousarray(np.asarray(L0_attn_X, dtype=np.float32))
    WQ = np.ascontiguousarray(np.asarray(L0_attn_W_Q, dtype=np.float32))
    WK = np.ascontiguousarray(np.asarray(L0_attn_W_K, dtype=np.float32))
    WV = np.ascontiguousarray(np.asarray(L0_attn_W_V, dtype=np.float32))
    w1 = np.ascontiguousarray(np.asarray(L0_ffn_w1, dtype=np.float32))
    w2 = np.ascontiguousarray(np.asarray(L0_ffn_w2, dtype=np.float32))
    cw = np.asarray(L0_ffn_conv_w, dtype=np.float32)
    cb = np.asarray(L0_ffn_conv_b, dtype=np.float32)
    cwm = np.ascontiguousarray(cw[:, 0, 0].reshape(32, 128).T)
    cw0 = np.ascontiguousarray(cw[:, 0, 1].reshape(32, 128).T)
    cwp = np.ascontiguousarray(cw[:, 0, 2].reshape(32, 128).T)
    cbt = np.ascontiguousarray(cb.reshape(32, 128).T)

    nc = _get_nc()
    in_maps = []
    for c in range(8):
        in_maps.append(dict(
            x_in=np.ascontiguousarray(X[2 * c:2 * c + 2]),
            wq_in=WQ, wk_in=WK, wv_in=WV, w1_in=w1, w2_in=w2,
            cwm_in=cwm, cw0_in=cw0, cwp_in=cwp, cb_in=cbt,
        ))
    res = run_bass_kernel_spmd(nc, in_maps, core_ids=list(range(8)))
    out = np.concatenate([r["out_o"] for r in res.results], axis=0)
    return np.ascontiguousarray(out.astype(np.float32))


# version 02
# speedup vs baseline: 1.4575x; 1.4575x over previous
import numpy as np
import ml_dtypes
from contextlib import ExitStack

import concourse.bass as bass
import concourse.mybir as mybir
from concourse.bass_utils import run_bass_kernel_spmd

F32 = mybir.dt.float32
BF16 = mybir.dt.bfloat16
AF = mybir.ActivationFunctionType
ALU = mybir.AluOpType
AX = mybir.AxisListType

MAGIC = float(np.float32(2 ** 23 + 2 ** 22))
T, D, HID = 4096, 1024, 4096
L, NCH = 512, 8
SLAB, NSLAB = 256, 16
NT = T // 128
NSEQ = 2
G64 = 0.99
EPS = 1e-5

DBG = False
PHASES = ("P0", "A", "B1", "B2a", "B2b")
A_STAGES = 6


class Ctr:

    def __init__(self, nc, ctx, name):
        self.h = ctx.enter_context(nc.semaphore(name))
        self.v = 0

    def inc(self, inst, n=1):
        inst.then_inc(self.h, n)
        self.v += n
        return self.v

    def wait(self, eng, v=None):
        if v is None:
            v = self.v
        if v <= 0:
            return
        assert v <= self.v, f"forward wait {v} > {self.v}"
        eng.wait_ge(self.h, v)


def _gp(e):
    return np.float32(G64 ** np.float64(e))


def build_consts():
    gamq = np.zeros((128, L), dtype=np.float32)
    for u in range(L):
        gamq[:, u] = _gp(u + 1)
    msk = np.zeros((128, 4, L), dtype=np.float32)
    for i in range(4):
        for p in range(128):
            s = 128 * i + p
            msk[p, i, s:] = _gp(-(s + 1))
    gamk = np.zeros((128, 4), dtype=np.float32)
    for i in range(4):
        for p in range(128):
            gamk[p, i] = _gp(511 - (128 * i + p))
    return gamq, msk, gamk


def build_nc():
    nc = bass.Bass("TRN2", target_bir_lowering=False)
    K = "ExternalOutput" if DBG else "Internal"

    x_in = nc.dram_tensor("x_in", [NSEQ, T, D], F32, kind="ExternalInput").ap()
    wq_in = nc.dram_tensor("wq_in", [D, D], F32, kind="ExternalInput").ap()
    wk_in = nc.dram_tensor("wk_in", [D, D], F32, kind="ExternalInput").ap()
    wv_in = nc.dram_tensor("wv_in", [D, D], F32, kind="ExternalInput").ap()
    w1_in = nc.dram_tensor("w1_in", [HID, D], F32, kind="ExternalInput").ap()
    w2_in = nc.dram_tensor("w2_in", [D, HID], F32, kind="ExternalInput").ap()
    cwm_in = nc.dram_tensor("cwm_in", [128, 32], F32, kind="ExternalInput").ap()
    cw0_in = nc.dram_tensor("cw0_in", [128, 32], F32, kind="ExternalInput").ap()
    cwp_in = nc.dram_tensor("cwp_in", [128, 32], F32, kind="ExternalInput").ap()
    cb_in = nc.dram_tensor("cb_in", [128, 32], F32, kind="ExternalInput").ap()
    out_o = nc.dram_tensor("out_o", [NSEQ, T, D], F32, kind="ExternalOutput").ap()

    xbf = nc.dram_tensor("xbf", [NSEQ, T, D], BF16, kind=K).ap()
    wqb = nc.dram_tensor("wqb", [D, D], BF16, kind=K).ap()
    wkb = nc.dram_tensor("wkb", [D, D], BF16, kind=K).ap()
    wvb = nc.dram_tensor("wvb", [D, D], BF16, kind=K).ap()
    wq1s = nc.dram_tensor("wq1s", [HID, D], BF16, kind=K).ap()
    wq2s = nc.dram_tensor("wq2s", [D, HID], BF16, kind=K).ap()
    ots = nc.dram_tensor("ots", [D, T], BF16, kind=K).ap()
    x2s = nc.dram_tensor("x2s", [T, D], F32, kind=K).ap()
    k1s = nc.dram_tensor("k1s", [T, D], BF16, kind=K).ap()
    xq2s = nc.dram_tensor("xq2s", [HID, T], BF16, kind=K).ap()

    gamq_np, msk_np, gamk_np = build_consts()
    i_gamq = nc.inline_tensor(gamq_np, "i_gamq")
    i_msk = nc.inline_tensor(msk_np, "i_msk")
    i_gamk = nc.inline_tensor(gamk_np, "i_gamk")
    i_identb = nc.inline_tensor(np.eye(128, dtype=ml_dtypes.bfloat16), "i_identb")
    i_identf = nc.inline_tensor(np.eye(128, dtype=np.float32), "i_identf")
    i_ones = nc.inline_tensor(np.ones((1, 128), dtype=np.float32), "i_ones")
    i_onescol = nc.inline_tensor(np.ones((128, 1), dtype=np.float32), "i_onescol")

    ctx0 = ExitStack()

    def sb(name, shape, dt, c=ctx0):
        return c.enter_context(nc.sbuf_tensor(name, shape, dt))

    def pst(name, shape, dt, c):
        return c.enter_context(nc.psum_tensor(name, shape, dt))

    c_gamq = sb("c_gamq", [128, L], F32)
    c_msk = sb("c_msk", [128, 4, L], F32)
    c_gamk = sb("c_gamk", [128, 4], F32)
    c_idb = sb("c_idb", [128, 128], BF16)
    c_idf = sb("c_idf", [128, 128], F32)
    c_ones = sb("c_ones", [1, 128], F32)
    c_onescol = sb("c_onescol", [128, 1], F32)
    c_cwm = sb("c_cwm", [128, 32], F32)
    c_cw0 = sb("c_cw0", [128, 32], F32)
    c_cwp = sb("c_cwp", [128, 32], F32)
    c_cb = sb("c_cb", [128, 32], F32)
    g_bsw1 = sb("g_bsw1", [128, 1], F32)
    g_bsw2 = sb("g_bsw2", [128, 1], F32)
    g_mw1 = sb("g_mw1", [128, 1], F32)
    g_mw2 = sb("g_mw2", [128, 1], F32)
    M2all = sb("M2all", [128, 32], F32)

    sLD = Ctr(nc, ctx0, "sLD")
    sST = Ctr(nc, ctx0, "sST")
    sCG = Ctr(nc, ctx0, "sCG")
    sT = Ctr(nc, ctx0, "sT")
    sV = Ctr(nc, ctx0, "sV")
    sS = Ctr(nc, ctx0, "sS")
    sG = Ctr(nc, ctx0, "sG")

    tens, vec, scal, sync, gps = nc.tensor, nc.vector, nc.scalar, nc.sync, nc.gpsimd

    for src, dst in [(i_gamq.ap(), c_gamq[:]), (i_msk.ap(), c_msk[:]),
                     (i_gamk.ap(), c_gamk[:]), (i_identb.ap(), c_idb[:]),
                     (i_identf.ap(), c_idf[:]), (i_ones.ap(), c_ones[:]),
                     (i_onescol.ap(), c_onescol[:]),
                     (cwm_in, c_cwm[:]), (cw0_in, c_cw0[:]),
                     (cwp_in, c_cwp[:]), (cb_in, c_cb[:])]:
        sLD.inc(sync.dma_start(out=dst, in_=src), 16)
    for s in range(NSEQ):
        sCG.inc(gps.dma_start(out=xbf[s], in_=x_in[s]), 16)
    for src, dst in [(wq_in, wqb), (wk_in, wkb), (wv_in, wvb)]:
        sCG.inc(gps.dma_start(out=dst, in_=src), 16)

    ctxP = ExitStack()
    wtile = [sb(f"p_wt{i}", [128, HID], F32, ctxP) for i in range(2)]
    wqout = [sb(f"p_wq{i}", [128, HID], BF16, ctxP) for i in range(2)]
    trash = sb("p_trash", [128, HID], F32, ctxP)
    SAB = sb("p_sab", [128, 32], F32, ctxP)
    redv = sb("p_red", [128, 1], F32, ctxP)
    sc1 = sb("p_sc1", [1, 8], F32, ctxP)
    pR = pst("p_pR", [1, 1], F32, ctxP)
    pB = pst("p_pB", [128, 2], F32, ctxP)

    def quant_weights(w_src, wq_dst, ntile, tilecols, nelem, b_sw, b_mw):
        sST.wait(scal)
        nc.all_engine_barrier()
        absdone = {}
        ldev = {}
        for i in range(ntile):
            if i >= 2:
                sS.wait(sync, absdone[i - 2])
            sLD.inc(sync.dma_start(out=wtile[i % 2][:, 0:tilecols],
                                   in_=w_src[128 * i:128 * (i + 1), :]), 16)
            ldev[i] = sLD.v
            sLD.wait(scal, ldev[i])
            absdone[i] = sS.inc(scal.activation(out=trash[:, 0:tilecols],
                                                in_=wtile[i % 2][:, 0:tilecols],
                                                func=AF.Abs, accum_out=SAB[:, i:i + 1]))
        sS.wait(vec)
        vec.reduce_sum(out=redv[:], in_=SAB[:, 0:ntile], axis=AX.X)
        vec.drain()
        ev = sV.inc(vec.engine_nop())
        sV.wait(tens, ev)
        tev = sT.inc(tens.matmul(out=pR[:], lhsT=redv[:], rhs=c_onescol[:], start=True, stop=True))
        sT.wait(scal, tev)
        sev = sS.inc(scal.copy(out=sc1[:, 0:1], in_=pR[:]))
        sS.wait(vec, sev)
        vec.tensor_scalar(out=sc1[:, 1:2], in0=sc1[:, 0:1], scalar1=1.0 / nelem,
                          scalar2=1e-5, op0=ALU.mult, op1=ALU.max)
        vec.drain()
        vec.tensor_scalar(out=sc1[:, 2:3], in0=sc1[:, 1:2], scalar1=1.0 / 127.0, scalar2=None,
                          op0=ALU.mult)
        vec.drain()
        ev = sV.inc(vec.engine_nop())
        sV.wait(tens, ev)
        tev = sT.inc(tens.matmul(out=pB[:, 0:2], lhsT=c_ones[:], rhs=sc1[0:1, 1:3],
                                 start=True, stop=True))
        sT.wait(scal, tev)
        scal.copy(out=b_sw[:], in_=pB[:, 0:1])
        sev = sS.inc(scal.copy(out=b_mw[:], in_=pB[:, 1:2]))
        sS.wait(vec, sev)
        vec.reciprocal(out=redv[:], in_=b_sw[:])
        vec.drain()
        vec.tensor_tensor(out=SAB[:, 0:1], in0=b_sw[:], in1=redv[:], op=ALU.mult)
        vec.drain()
        vec.tensor_scalar(out=SAB[:, 0:1], in0=SAB[:, 0:1], scalar1=-1.0, scalar2=2.0,
                          op0=ALU.mult, op1=ALU.add)
        vec.drain()
        nev = sV.inc(vec.tensor_tensor(out=b_sw[:], in0=redv[:], in1=SAB[:, 0:1], op=ALU.mult))
        sV.wait(scal, nev)
        sev = sS.inc(scal.copy(out=trash[0:1, 0:1], in_=trash[0:1, 0:1]))
        qev = {}
        dmev = {}
        for i in range(ntile):
            if i >= 2:
                sV.wait(sync, qev[i - 2])
            sLD.inc(sync.dma_start(out=wtile[i % 2][:, 0:tilecols],
                                   in_=w_src[128 * i:128 * (i + 1), :]), 16)
            ld2 = sLD.v
            sLD.wait(vec, ld2)
            if i == 0:
                sS.wait(vec, sev)
            if i >= 2:
                sST.wait(vec, dmev[i - 2])
            vec.tensor_scalar(out=trash[:, 0:tilecols], in0=wtile[i % 2][:, 0:tilecols],
                              scalar1=b_sw[:, 0:1], scalar2=MAGIC, op0=ALU.mult, op1=ALU.add)
            vec.tensor_scalar(out=trash[:, 0:tilecols], in0=trash[:, 0:tilecols],
                              scalar1=MAGIC, scalar2=1.0, op0=ALU.subtract, op1=ALU.min)
            qev[i] = sV.inc(vec.tensor_scalar(out=wqout[i % 2][:, 0:tilecols], in0=trash[:, 0:tilecols],
                                              scalar1=-1.0, scalar2=None, op0=ALU.max))
            sV.wait(scal, qev[i])
            dmev[i] = sST.inc(scal.dma_start(out=wq_dst[128 * i:128 * (i + 1), :],
                                             in_=wqout[i % 2][:, 0:tilecols]), 16)
        return dmev[ntile - 1]

    if "P0" in PHASES:
        quant_weights(w1_in, wq1s, HID // 128, D, float(HID) * D, g_bsw1, g_mw1)
        quant_weights(w2_in, wq2s, D // 128, HID, float(HID) * D, g_bsw2, g_mw2)
        if DBG:
            dbgo = nc.dram_tensor("dbgo", [128, 4], F32, kind="ExternalOutput").ap()
            for col, src_t in enumerate([g_bsw1, g_mw1, g_bsw2, g_mw2]):
                sST.inc(scal.dma_start(out=dbgo[:, col:col + 1], in_=src_t[:], allow_slow_non_contiguous=True), 16)
            dbgo2 = nc.dram_tensor("dbgo2", [128, 34], F32, kind="ExternalOutput").ap()
            sST.inc(scal.dma_start(out=dbgo2[:, 0:32], in_=SAB[:, 0:32]), 16)
            sST.inc(scal.dma_start(out=dbgo2[:, 32:33], in_=redv[:], allow_slow_non_contiguous=True), 16)
            sST.inc(scal.dma_start(out=dbgo2[:, 33:34], in_=g_bsw2[:], allow_slow_non_contiguous=True), 16)
    sST.wait(scal)
    sLD.wait(sync)
    ctxP.close()
    nc.all_engine_barrier()

    DUMP_B1 = None
    DUMP_B2 = None
    if DBG:
        DUMP_B1 = nc.dram_tensor("dbg_b1", [128, 12], F32, kind="ExternalOutput").ap()
        DUMP_B2 = dict(
            xq1t=nc.dram_tensor("dbg_xq1t", [128, 8, 288], BF16, kind="ExternalOutput").ap(),
            h1=nc.dram_tensor("dbg_h1", [128, 32, 258], BF16, kind="ExternalOutput").ap(),
            cv=nc.dram_tensor("dbg_cv", [128, 32, 256], BF16, kind="ExternalOutput").ap(),
            s2b=nc.dram_tensor("dbg_s2b", [128, 256], F32, kind="ExternalOutput").ap(),
            macc=nc.dram_tensor("dbg_macc", [128, 256], F32, kind="ExternalOutput").ap(),
        )
    DBGA = None
    if DBG and globals().get("DUMP_A"):
        DBGA = dict(
            qh=nc.dram_tensor("dbg_qh", [128, 8, L], BF16, kind="ExternalOutput").ap(),
            kt=nc.dram_tensor("dbg_kt", [128, 8, L], BF16, kind="ExternalOutput").ap(),
            vt=nc.dram_tensor("dbg_vt", [128, 4, D], BF16, kind="ExternalOutput").ap(),
            at=nc.dram_tensor("dbg_at", [128, 4, L], BF16, kind="ExternalOutput").ap(),
            xt=nc.dram_tensor("dbg_xt", [128, 8, L], BF16, kind="ExternalOutput").ap(),
            ot0=nc.dram_tensor("dbg_ot0", [128, 8, L], BF16, kind="ExternalOutput").ap(),
            s1=nc.dram_tensor("dbg_s1", [128, 8, D], BF16, kind="ExternalOutput").ap(),
        )
    for s in range(NSEQ):
        if "A" in PHASES:
            phase_A(nc, s, locals())
            nc.all_engine_barrier()
        if "B1" in PHASES:
            phase_B1(nc, s, locals())
            nc.all_engine_barrier()
        if "B2a" in PHASES:
            phase_B2a(nc, s, locals())
            nc.all_engine_barrier()
        if "B2b" in PHASES:
            phase_B2b(nc, s, locals())
            nc.all_engine_barrier()

    sST.wait(sync)
    sST.wait(scal)
    ctx0.close()
    return nc


def phase_A(nc, s, env):
    g = env
    sb_ = lambda n, sh, dt, c: c.enter_context(nc.sbuf_tensor(n + f"_s{s}", sh, dt))
    ps_ = lambda n, sh, dt, c: c.enter_context(nc.psum_tensor(n + f"_s{s}", sh, dt))
    tens, vec, scal, sync = nc.tensor, nc.vector, nc.scalar, nc.sync
    sLD, sST, sCG, sT, sV, sS = g["sLD"], g["sST"], g["sCG"], g["sT"], g["sV"], g["sS"]
    c_gamq, c_msk, c_gamk, c_idb = g["c_gamq"], g["c_msk"], g["c_gamk"], g["c_idb"]
    xbf, ots, wqb, wkb, wvb = g["xbf"], g["ots"], g["wqb"], g["wkb"], g["wvb"]

    ctx = ExitStack()
    aWQ = sb_("aWQ", [128, 8, D], BF16, ctx)
    aWK = sb_("aWK", [128, 8, D], BF16, ctx)
    aWV = sb_("aWV", [128, 8, D], BF16, ctx)
    aS = sb_("aS", [128, 8, D], BF16, ctx)
    aXT = [sb_(f"aXT{i}", [128, 8, L], BF16, ctx) for i in range(2)]
    aQH = sb_("aQH", [128, 8, L], BF16, ctx)
    aKT = sb_("aKT", [128, 8, L], BF16, ctx)
    aVT = sb_("aVT", [128, 4, D], BF16, ctx)
    aKTM = sb_("aKTM", [128, 4, D], BF16, ctx)
    aAT = sb_("aAT", [128, 4, L], BF16, ctx)
    aOT = [sb_(f"aOT{i}", [128, 8, L], BF16, ctx) for i in range(2)]
    pQV = [ps_(f"pQV{i}", [128, L], F32, ctx) for i in range(2)]
    pAT = ps_("pAT", [128, L], F32, ctx)
    pO = [ps_(f"pO{i}", [128, L], F32, ctx) for i in range(2)]
    pDS = [ps_(f"pDS{i}", [128, 512], F32, ctx) for i in range(2)]
    pTR = ps_("pTR", [128, 8, 128], BF16, ctx)

    sCG.wait(sync)
    for src, dst in [(wqb, aWQ), (wkb, aWK), (wvb, aWV)]:
        sLD.inc(sync.dma_start(out=dst[:], in_=src.rearrange("(dc p) c -> p dc c", p=128)), 16)
    ld_w = sLD.v
    ev_memset = sV.inc(vec.memset(aS[:], 0.0))

    ld_x = {}
    qkv_done = {}
    o_done = {}
    ds_done = {}
    attr_done = {}
    ot_dma = {}
    bank_guard = {}

    def loads(c):
        if c >= 2:
            sT.wait(sync, qkv_done[c - 2])
        for f in range(8):
            sLD.inc(sync.dma_start(out=aXT[c % 2][:, f, :],
                                   in_=xbf[s, L * c:L * (c + 1), 128 * f:128 * (f + 1)],
                                   transpose=True), 16)
        ld_x[c] = sLD.v

    loads(0)
    loads(1)

    for c in range(NCH):
        par = c % 2
        sLD.wait(tens, ld_x[c])
        if c == 0:
            sLD.wait(tens, ld_w)
            sV.wait(tens, ev_memset)
        grp = 0
        if c > 0:
            sT.wait(vec, o_done[c - 1])
            sT.wait(scal, attr_done[c - 1])
            sT.wait(scal, ds_done[c - 1])
        for cc in range(8):
            b = pQV[grp % 2]
            if ("qv", grp % 2) in bank_guard:
                ctr, val = bank_guard[("qv", grp % 2)]
                ctr.wait(tens, val)
            for dc in range(8):
                mi = tens.matmul(out=b[:], lhsT=aWQ[:, dc, 128 * cc:128 * (cc + 1)],
                            rhs=aXT[par][:, dc, :], start=(dc == 0), stop=(dc == 7))
            tev = sT.inc(mi)
            sT.wait(vec, tev)
            vev = sV.inc(vec.tensor_tensor(out=aQH[:, cc, :], in0=b[:], in1=c_gamq[:], op=ALU.mult))
            bank_guard[("qv", grp % 2)] = (sV, vev)
            grp += 1
        for cc in range(8):
            b = pQV[grp % 2]
            ctr, val = bank_guard[("qv", grp % 2)]
            ctr.wait(tens, val)
            for dc in range(8):
                mi = tens.matmul(out=b[:], lhsT=aWK[:, dc, 128 * cc:128 * (cc + 1)],
                            rhs=aXT[par][:, dc, :], start=(dc == 0), stop=(dc == 7))
            tev = sT.inc(mi)
            sT.wait(scal, tev)
            sev = sS.inc(scal.copy(out=aKT[:, cc, :], in_=b[:]))
            bank_guard[("qv", grp % 2)] = (sS, sev)
            grp += 1
        for ts_ in range(4):
            for hf in range(2):
                b = pQV[grp % 2]
                ctr, val = bank_guard[("qv", grp % 2)]
                ctr.wait(tens, val)
                for dc in range(8):
                    mi = tens.matmul(out=b[:], lhsT=aXT[par][:, dc, 128 * ts_:128 * (ts_ + 1)],
                                rhs=aWV[:, dc, 512 * hf:512 * (hf + 1)],
                                start=(dc == 0), stop=(dc == 7))
                tev = sT.inc(mi)
                sT.wait(scal, tev)
                sev = sS.inc(scal.copy(out=aVT[:, ts_, 512 * hf:512 * (hf + 1)], in_=b[:]))
                bank_guard[("qv", grp % 2)] = (sS, sev)
                grp += 1
        qkv_done[c] = sT.v
        kt_all = sS.v
        qh_all = sV.v

        _AS = globals().get('A_STAGES', 6)
        if _AS >= 2:
            sS.wait(tens, kt_all)
            if c > 0:
                sT.wait(vec, ds_done[c - 1])
            for i in range(4):
                if ("tr",) in bank_guard:
                    ctr, val = bank_guard[("tr",)]
                    ctr.wait(tens, val)
                for cc in range(8):
                    mi = tens.transpose(out=pTR[:, cc, :], in_=aKT[:, cc, 128 * i:128 * (i + 1)],
                                   identity=c_idb[:])
                tev = sT.inc(mi)
                sT.wait(vec, tev)
                vev = sV.inc(vec.tensor_scalar(out=aKTM[:, i, :], in0=pTR[:].rearrange("p a b -> p (a b)"),
                                               scalar1=c_gamk[:, i:i + 1], scalar2=None, op0=ALU.mult))
                bank_guard[("tr",)] = (sV, vev)
        ktm_all = sV.v
        if _AS >= 3:
            sV.wait(tens, qh_all)
            if c > 0:
                sT.wait(vec, o_done[c - 1])
            for i in range(4):
                if ("at",) in bank_guard:
                    ctr, val = bank_guard[("at",)]
                    ctr.wait(tens, val)
                n0 = 128 * i
                for cc in range(8):
                    mi = tens.matmul(out=pAT[:, n0:L], lhsT=aKT[:, cc, 128 * i:128 * (i + 1)],
                                rhs=aQH[:, cc, n0:L], start=(cc == 0), stop=(cc == 7))
                tev = sT.inc(mi)
                sT.wait(vec, tev)
                vev = sV.inc(vec.tensor_tensor(out=aAT[:, i, n0:L], in0=pAT[:, n0:L],
                                               in1=c_msk[:, i, n0:L], op=ALU.mult))
                bank_guard[("at",)] = (sV, vev)
            at_all = sV.v
        at_all = sV.v
        attr_done[c] = sT.v
        if _AS >= 4:
            sV.wait(tens, at_all)
            if c > 0:
                sST.wait(scal, ot_dma.get(c - 2, 0))
            for cc in range(8):
                b = pO[cc % 2]
                if ("o", cc % 2) in bank_guard:
                    ctr, val = bank_guard[("o", cc % 2)]
                    ctr.wait(tens, val)
                for i in range(4):
                    n0 = 128 * i
                    mi = tens.matmul(out=b[:, n0:L], lhsT=aVT[:, i, 128 * cc:128 * (cc + 1)],
                                rhs=aAT[:, i, n0:L], start=(i == 0), stop=False)
                for ck in range(8):
                    mi = tens.matmul(out=b[:], lhsT=aS[:, ck, 128 * cc:128 * (cc + 1)],
                                rhs=aQH[:, ck, :], start=False, stop=(ck == 7))
                tev = sT.inc(mi)
                sT.wait(scal, tev)
                sev = sS.inc(scal.copy(out=aOT[par][:, cc, :], in_=b[:]))
                bank_guard[("o", cc % 2)] = (sS, sev)
            o_done[c] = sT.v
            o_evac = sS.v
        o_done[c] = sT.v
        if _AS >= 5:
            sV.wait(tens, ktm_all)
            for ck in range(8):
                for hf in range(2):
                    if ("ds", hf) in bank_guard:
                        ctr, val = bank_guard[("ds", hf)]
                        ctr.wait(tens, val)
                    b = pDS[hf]
                    for i in range(4):
                        mi = tens.matmul(out=b[:], lhsT=aKTM[:, i, 128 * ck:128 * (ck + 1)],
                                    rhs=aVT[:, i, 512 * hf:512 * (hf + 1)],
                                    start=(i == 0), stop=(i == 3))
                    tev = sT.inc(mi)
                    sT.wait(vec, tev)
                    vev = sV.inc(vec.scalar_tensor_tensor(
                        out=aS[:, ck, 512 * hf:512 * (hf + 1)],
                        in0=aS[:, ck, 512 * hf:512 * (hf + 1)],
                        scalar=float(_gp(512)), in1=b[:], op0=ALU.mult, op1=ALU.add))
                    bank_guard[("ds", hf)] = (sV, vev)
        ds_done[c] = sT.v
        if _AS >= 6:
            for gg in range(8):
                ot_dma[c] = sST.inc(scal.dma_start(
                    out=ots[128 * gg:128 * (gg + 1), L * c:L * (c + 1)],
                    in_=aOT[par][:, gg, :]), 16)
        else:
            ot_dma[c] = 0
        if g.get("DBGA") is not None and s == 0 and c == 0:
            dbga = g["DBGA"]
            sV.wait(scal)
            sS.wait(scal)
            sST.inc(scal.dma_start(out=dbga["qh"], in_=aQH[:]), 16)
            sST.inc(scal.dma_start(out=dbga["kt"], in_=aKT[:]), 16)
            sST.inc(scal.dma_start(out=dbga["vt"], in_=aVT[:]), 16)
            sST.inc(scal.dma_start(out=dbga["at"], in_=aAT[:]), 16)
            sST.inc(scal.dma_start(out=dbga["xt"], in_=aXT[par][:]), 16)
            sST.inc(scal.dma_start(out=dbga["ot0"], in_=aOT[par][:]), 16)
            sST.inc(scal.dma_start(out=dbga["s1"], in_=aS[:]), 16)
            sST.wait(scal)
            sST.wait(vec)
            sST.wait(sync)
            sST.wait(tens)

        if c + 2 < NCH:
            loads(c + 2)

    sST.wait(scal)
    sLD.wait(sync)
    ctx.close()


def phase_B1(nc, s, env):
    g = env
    sb_ = lambda n, sh, dt, c: c.enter_context(nc.sbuf_tensor(n + f"_b1s{s}", sh, dt))
    tens, vec, scal, sync = nc.tensor, nc.vector, nc.scalar, nc.sync
    sLD, sST, sT, sV, sS = g["sLD"], g["sST"], g["sT"], g["sV"], g["sS"]
    x_in, ots, x2s, k1s, g_mw1 = g["x_in"], g["ots"], g["x2s"], g["k1s"], g["g_mw1"]

    ctx = ExitStack()
    xt = [sb_(f"xt{i}", [128, D], F32, ctx) for i in range(2)]
    ot = [sb_(f"ot{i}", [128, D], BF16, ctx) for i in range(2)]
    rW = [sb_(f"rW{i}", [128, D], F32, ctx) for i in range(2)]
    x2t = [sb_(f"x2t{i}", [128, D], F32, ctx) for i in range(2)]
    k1t = [sb_(f"k1t{i}", [128, D], BF16, ctx) for i in range(2)]
    tmpD = sb_("tmpD", [128, D], F32, ctx)
    trash = sb_("trashb1", [128, D], F32, ctx)
    st_ = sb_("stats", [128, 12], F32, ctx)
    ot_flat = ots.flatten()

    ld = {}
    ev_r = {}
    ev_x2dma = {}
    ev_k1dma = {}
    ev_sq = {}
    ev_done = {}

    def loads(j):
        if j >= 2:
            sV.wait(sync, ev_done[j - 2])
        sLD.inc(sync.dma_start(out=xt[j % 2][:], in_=x_in[s, 128 * j:128 * (j + 1), :]), 16)
        sLD.inc(sync.dma_start(out=ot[j % 2][:],
                               in_=ot_flat[128 * D * j:128 * D * (j + 1)].rearrange("(p t) -> p t", p=128)), 16)
        ld[j] = sLD.v

    loads(0)
    loads(1)
    for j in range(NT):
        par = j % 2
        sLD.wait(vec, ld[j])
        if j >= 2:
            sS.wait(vec, ev_sq[j - 2])
            sST.wait(vec, ev_x2dma[j - 2])
            sST.wait(vec, ev_k1dma[j - 2])
        ev_r[j] = sV.inc(vec.tensor_tensor(out=rW[par][:], in0=xt[par][:], in1=ot[par][:], op=ALU.add))
        vec.reduce_sum(out=st_[:, 0:1], in_=rW[par][:], axis=AX.X)
        vec.drain()
        sV.wait(scal, ev_r[j])
        ev_sq[j] = sS.inc(scal.activation(out=trash[:], in_=rW[par][:], func=AF.Square, accum_out=st_[:, 1:2]))
        sS.wait(vec, ev_sq[j])
        vec.tensor_scalar(out=st_[:, 2:3], in0=st_[:, 0:1], scalar1=1.0 / D, scalar2=None, op0=ALU.mult)
        vec.tensor_scalar(out=st_[:, 3:4], in0=st_[:, 1:2], scalar1=1.0 / D, scalar2=None, op0=ALU.mult)
        vec.drain()
        vec.scalar_tensor_tensor(out=st_[:, 4:5], in0=st_[:, 2:3], scalar=st_[:, 2:3],
                                 in1=st_[:, 3:4], op0=ALU.mult, op1=ALU.subtract)
        vec.drain()
        vec.tensor_scalar(out=st_[:, 5:6], in0=st_[:, 4:5], scalar1=-1.0, scalar2=EPS,
                          op0=ALU.mult, op1=ALU.add)
        vec.drain()
        ev = sV.inc(vec.reciprocal(out=st_[:, 6:7], in_=st_[:, 5:6]))
        sV.wait(scal, ev)
        ev = sS.inc(scal.activation(out=st_[:, 7:8], in_=st_[:, 6:7], func=AF.Sqrt))
        sS.wait(vec, ev)
        vec.tensor_tensor(out=st_[:, 8:9], in0=st_[:, 2:3], in1=st_[:, 7:8], op=ALU.mult)
        vec.drain()
        vec.tensor_scalar(out=st_[:, 8:9], in0=st_[:, 8:9], scalar1=-1.0, scalar2=None, op0=ALU.mult)
        vec.drain()
        ev_x2 = sV.inc(vec.tensor_scalar(out=x2t[par][:], in0=rW[par][:], scalar1=st_[:, 7:8],
                                         scalar2=st_[:, 8:9], op0=ALU.mult, op1=ALU.add))
        sV.wait(scal, ev_x2)
        ev_x2dma[j] = sST.inc(scal.dma_start(out=x2s[128 * j:128 * (j + 1), :], in_=x2t[par][:]), 16)
        vec.reduce_max(out=st_[:, 9:10], in_=x2t[par][:], axis=AX.X, apply_absolute_value=True)
        vec.drain()
        vec.tensor_scalar(out=st_[:, 9:10], in0=st_[:, 9:10], scalar1=1e-5, scalar2=None, op0=ALU.max)
        vec.drain()
        vec.reciprocal(out=st_[:, 10:11], in_=st_[:, 9:10])
        vec.drain()
        vec.tensor_tensor(out=st_[:, 11:12], in0=st_[:, 9:10], in1=st_[:, 10:11], op=ALU.mult)
        vec.drain()
        vec.tensor_scalar(out=st_[:, 11:12], in0=st_[:, 11:12], scalar1=-1.0, scalar2=2.0,
                          op0=ALU.mult, op1=ALU.add)
        vec.drain()
        vec.tensor_tensor(out=st_[:, 10:11], in0=st_[:, 10:11], in1=st_[:, 11:12], op=ALU.mult)
        vec.drain()
        vec.tensor_scalar(out=st_[:, 11:12], in0=st_[:, 10:11], scalar1=127.0, scalar2=None, op0=ALU.mult)
        vec.tensor_tensor(out=st_[:, 9:10], in0=st_[:, 9:10], in1=g_mw1[:], op=ALU.mult)
        vec.drain()
        vec.tensor_scalar(out=tmpD[:], in0=x2t[par][:], scalar1=st_[:, 11:12], scalar2=MAGIC,
                          op0=ALU.mult, op1=ALU.add)
        ev_k1 = sV.inc(vec.tensor_scalar(out=k1t[par][:], in0=tmpD[:], scalar1=MAGIC,
                                         scalar2=st_[:, 9:10], op0=ALU.subtract, op1=ALU.mult))
        ev_done[j] = ev_k1
        sV.wait(scal, ev_k1)
        ev_k1dma[j] = sST.inc(scal.dma_start(out=k1s[128 * j:128 * (j + 1), :], in_=k1t[par][:]), 16)
        if j + 2 < NT:
            loads(j + 2)

    if g.get("DUMP_B1") is not None:
        sV.wait(scal)
        sST.inc(scal.dma_start(out=g["DUMP_B1"], in_=st_[:]), 16)
    sST.wait(scal)
    sLD.wait(sync)
    ctx.close()


def phase_B2a(nc, s, env):
    g = env
    sb_ = lambda n, sh, dt, c: c.enter_context(nc.sbuf_tensor(n + f"_b2as{s}", sh, dt))
    ps_ = lambda n, sh, dt, c: c.enter_context(nc.psum_tensor(n + f"_b2as{s}", sh, dt))
    tens, vec, scal, sync, gps = nc.tensor, nc.vector, nc.scalar, nc.sync, nc.gpsimd
    sLD, sST, sT, sV, sS, sG = g["sLD"], g["sST"], g["sT"], g["sV"], g["sS"], g["sG"]
    k1s, xq2s, wq1s = g["k1s"], g["xq2s"], g["wq1s"]
    M2all = g["M2all"]
    c_cwm, c_cw0, c_cwp, c_cb = g["c_cwm"], g["c_cw0"], g["c_cwp"], g["c_cb"]
    c_idf, c_ones = g["c_idf"], g["c_ones"]

    ctx = ExitStack()
    WQ1T = sb_("WQ1T", [128, 8, HID], BF16, ctx)
    XQ1T = [sb_(f"XQ1T{i}", [128, 8, 288], BF16, ctx) for i in range(2)]
    H1 = [sb_(f"H1_{i}", [128, 32, 258], BF16, ctx) for i in range(2)]
    CV = sb_("CV", [128, 32, 256], BF16, ctx)
    XQ2 = [sb_(f"XQ2_{i}", [128, 32, SLAB], BF16, ctx) for i in range(2)]
    TMP = [sb_(f"TMP{i}", [128, SLAB], F32, ctx) for i in range(2)]
    macc = sb_("macc", [128, SLAB], F32, ctx)
    S2B = sb_("S2B", [128, SLAB], F32, ctx)
    s2t = sb_("s2t", [128, 2], F32, ctx)
    s2w = sb_("s2w", [128, 2], F32, ctx)
    s2row = sb_("s2row", [1, 256], F32, ctx)
    pH = [ps_(f"pH{i}", [128, 258], F32, ctx) for i in range(2)]
    pM = ps_("pM", [128, 2, 128], F32, ctx)
    pM2 = ps_("pM2", [1, 256], F32, ctx)
    pS2 = ps_("pS2", [128, SLAB], F32, ctx)

    for dc in range(8):
        sLD.inc(sync.dma_start(out=WQ1T[:, dc, :], in_=wq1s[:, 128 * dc:128 * (dc + 1)],
                               transpose=True), 16)
    ld_w = sLD.v

    ld = {}
    mm1_done = {}
    halo = {}
    qdone = {}
    xq2dma = {}
    silu_ev = {}
    bank = {}

    def loads(sl):
        if sl >= 2:
            sT.wait(sync, mm1_done[sl - 2])
            sV.wait(sync, halo[sl - 1])
        t0 = SLAB * sl
        rows = 272 if sl < NSLAB - 1 else 256
        for dc in range(8):
            sLD.inc(sync.dma_start(out=XQ1T[sl % 2][:, dc, 16:16 + rows],
                                   in_=k1s[t0:t0 + rows, 128 * dc:128 * (dc + 1)],
                                   transpose=True), 16)
        ld[sl] = sLD.v

    loads(0)
    for sl in range(NSLAB):
        par = sl % 2
        if sl == 0:
            sLD.wait(vec, ld[0])
            vec.memset(XQ1T[0][:, :, 15:16], 0.0)
            vec.drain()
            halo[0] = sV.inc(vec.engine_nop())
        if sl + 1 < NSLAB:
            loads(sl + 1)
            sLD.wait(vec, ld[sl])
            if sl >= 1:
                sT.wait(vec, mm1_done[sl - 1])
            vec.tensor_copy(out=XQ1T[1 - par][:, :, 15:16], in_=XQ1T[par][:, :, 271:272])
            if sl + 1 == NSLAB - 1:
                vec.memset(XQ1T[1 - par][:, :, 272:273], 0.0)
            vec.drain()
            halo[sl + 1] = sV.inc(vec.engine_nop())
        sV.wait(tens, halo[sl])
        sLD.wait(tens, ld[sl])
        if sl == 0:
            sLD.wait(tens, ld_w)
        ev_evac = {}
        for hc in range(32):
            if ("h", hc % 2) in bank:
                ctr, val = bank[("h", hc % 2)]
                ctr.wait(tens, val)
            b = pH[hc % 2]
            for dc in range(8):
                mi = tens.matmul(out=b[:], lhsT=WQ1T[:, dc, 128 * hc:128 * (hc + 1)],
                            rhs=XQ1T[par][:, dc, 15:273], start=(dc == 0), stop=(dc == 7))
            tev = sT.inc(mi)
            sT.wait(scal, tev)
            if hc == 0 and sl >= 2:
                sV.wait(scal, qdone[sl - 2])
            sev = sS.inc(scal.copy(out=H1[par][:, hc, :], in_=b[:]))
            ev_evac[hc] = sev
            bank[("h", hc % 2)] = (sS, sev)
        mm1_done[sl] = sT.v
        for hc in range(32):
            sS.wait(vec, ev_evac[hc])
            if hc == 0 and sl >= 1:
                sS.wait(vec, silu_ev[sl - 1])
            vec.tensor_scalar(out=CV[:, hc, :], in0=H1[par][:, hc, 1:257],
                              scalar1=c_cw0[:, hc:hc + 1], scalar2=c_cb[:, hc:hc + 1],
                              op0=ALU.mult, op1=ALU.add)
            vec.scalar_tensor_tensor(out=CV[:, hc, :], in0=H1[par][:, hc, 0:256],
                                     scalar=c_cwm[:, hc:hc + 1], in1=CV[:, hc, :],
                                     op0=ALU.mult, op1=ALU.add)
            vec.scalar_tensor_tensor(out=CV[:, hc, :], in0=H1[par][:, hc, 2:258],
                                     scalar=c_cwp[:, hc:hc + 1], in1=CV[:, hc, :],
                                     op0=ALU.mult, op1=ALU.add)
        conv_ev = sV.inc(vec.tensor_copy(out=CV[:, 0:1, 0:1], in_=CV[:, 0:1, 0:1]))
        sV.wait(scal, conv_ev)
        silu_ev[sl] = sS.inc(scal.activation(out=H1[par][:, :, 0:256],
                                             in_=CV[:].rearrange("p a b -> p (a b)"), func=AF.Silu))
        sS.wait(vec, silu_ev[sl])
        mev = sV.inc(vec.reduce_max(out=macc[:], in_=H1[par][:, :, 0:256].rearrange("p hc t -> p t hc"),
                                    axis=AX.X, apply_absolute_value=True))
        sV.wait(tens, mev)
        if ("m",) in bank:
            ctr, val = bank[("m",)]
            ctr.wait(tens, val)
        mi = tens.transpose(out=pM[:, 0, :], in_=macc[:, 0:128], identity=c_idf[:])
        mi = tens.transpose(out=pM[:, 1, :], in_=macc[:, 128:256], identity=c_idf[:])
        tev = sT.inc(mi)
        sT.wait(vec, tev)
        for jj in range(2):
            vec.reduce_max(out=s2t[:, jj:jj + 1], in_=pM[:, jj, :], axis=AX.X)
        vec.drain()
        vev = sV.inc(vec.tensor_scalar(out=s2t[:], in0=s2t[:], scalar1=1e-5, scalar2=None, op0=ALU.max))
        bank[("m",)] = (sV, vev)
        vec.drain()
        vec.tensor_copy(out=M2all[:, 2 * sl:2 * sl + 2], in_=s2t[:])
        vec.reciprocal(out=s2w[:], in_=s2t[:])
        vec.drain()
        vec.tensor_tensor(out=s2t[:], in0=s2t[:], in1=s2w[:], op=ALU.mult)
        vec.drain()
        vec.tensor_scalar(out=s2t[:], in0=s2t[:], scalar1=-1.0, scalar2=2.0, op0=ALU.mult, op1=ALU.add)
        vec.drain()
        vec.tensor_tensor(out=s2w[:], in0=s2w[:], in1=s2t[:], op=ALU.mult)
        vec.drain()
        vev = sV.inc(vec.tensor_scalar(out=s2w[:], in0=s2w[:], scalar1=127.0, scalar2=None, op0=ALU.mult))
        sV.wait(tens, vev)
        if ("m2",) in bank:
            ctr, val = bank[("m2",)]
            ctr.wait(tens, val)
        mi = tens.transpose(out=pM2[0:1, 0:128], in_=s2w[:, 0:1], identity=c_idf[:])
        mi = tens.transpose(out=pM2[0:1, 128:256], in_=s2w[:, 1:2], identity=c_idf[:])
        tev = sT.inc(mi)
        sT.wait(scal, tev)
        sev = sS.inc(scal.copy(out=s2row[:], in_=pM2[:]))
        bank[("m2",)] = (sS, sev)
        sS.wait(tens, sev)
        if ("s2b",) in bank:
            ctr, val = bank[("s2b",)]
            ctr.wait(tens, val)
        mi = tens.matmul(out=pS2[:], lhsT=c_ones[:], rhs=s2row[0:1, :], start=True, stop=True)
        tev = sT.inc(mi)
        sT.wait(scal, tev)
        s2b_ev = sS.inc(scal.copy(out=S2B[:], in_=pS2[:]))
        bank[("s2b",)] = (sS, s2b_ev)
        sS.wait(vec, s2b_ev)
        if sl >= 2:
            sST.wait(vec, xq2dma[sl - 2])
        vec.tensor_tensor(out=TMP[0][:], in0=H1[par][:, 0, 0:256], in1=S2B[:], op=ALU.mult)
        for hc in range(1, 32):
            vec.tensor_tensor(out=TMP[hc % 2][:], in0=H1[par][:, hc, 0:256],
                              in1=S2B[:], op=ALU.mult)
            vec.tensor_scalar(out=XQ2[par][:, hc - 1, :], in0=TMP[(hc - 1) % 2][:],
                              scalar1=MAGIC, scalar2=MAGIC,
                              op0=ALU.add, op1=ALU.subtract)
        vec.tensor_scalar(out=XQ2[par][:, 31, :], in0=TMP[31 % 2][:],
                          scalar1=MAGIC, scalar2=MAGIC,
                          op0=ALU.add, op1=ALU.subtract)
        vec.drain()
        qdone[sl] = sV.inc(vec.engine_nop())
        sV.wait(scal, qdone[sl])
        for gg in range(32):
            xq2dma[sl] = sST.inc(scal.dma_start(
                out=xq2s[128 * gg:128 * (gg + 1), SLAB * sl:SLAB * (sl + 1)],
                in_=XQ2[par][:, gg, :]), 16)
    if g.get("DUMP_B2") is not None and s == 1:
        d2 = g["DUMP_B2"]
        sV.wait(scal)
        sG.wait(scal)
        sST.inc(scal.dma_start(out=d2["xq1t"], in_=XQ1T[1][:]), 16)
        sST.inc(scal.dma_start(out=d2["h1"], in_=H1[1][:]), 16)
        sST.inc(scal.dma_start(out=d2["cv"], in_=CV[:]), 16)
        sST.inc(scal.dma_start(out=d2["s2b"], in_=S2B[:]), 16)
        sST.inc(scal.dma_start(out=d2["macc"], in_=macc[:]), 16)
    sST.wait(scal)
    sLD.wait(sync)
    ctx.close()


def phase_B2b(nc, s, env):
    g = env
    sb_ = lambda n, sh, dt, c: c.enter_context(nc.sbuf_tensor(n + f"_b2bs{s}", sh, dt))
    ps_ = lambda n, sh, dt, c: c.enter_context(nc.psum_tensor(n + f"_b2bs{s}", sh, dt))
    tens, vec, scal, sync = nc.tensor, nc.vector, nc.scalar, nc.sync
    sLD, sST, sT, sV, sS = g["sLD"], g["sST"], g["sT"], g["sV"], g["sS"]
    xq2s, wq2s, x2s, out_o, g_mw2, M2all = g["xq2s"], g["wq2s"], g["x2s"], g["out_o"], g["g_mw2"], g["M2all"]

    ctx = ExitStack()
    WQ2T = sb_("WQ2T", [128, 32, D], BF16, ctx)
    XQI = [sb_(f"XQI{i}", [128, 32, 128], BF16, ctx) for i in range(2)]
    x2t = [sb_(f"x2t{i}", [128, D], F32, ctx) for i in range(2)]
    rW = [sb_(f"rW{i}", [128, D], F32, ctx) for i in range(2)]
    outt = [sb_(f"outt{i}", [128, D], F32, ctx) for i in range(2)]
    trash = sb_("trashb2", [128, D], F32, ctx)
    st_ = sb_("stats2", [128, 12], F32, ctx)
    pO2 = [ps_(f"pO2_{i}", [128, D], F32, ctx) for i in range(2)]

    for hc in range(32):
        sLD.inc(sync.dma_start(out=WQ2T[:, hc, :], in_=wq2s[:, 128 * hc:128 * (hc + 1)],
                               transpose=True), 16)
    ld_w = sLD.v

    ld = {}
    ev_r2 = {}
    ev_outdma = {}
    ev_sq = {}
    mm_done = {}

    def loads(j):
        if j >= 2:
            sT.wait(sync, mm_done[j - 2])
            sV.wait(sync, ev_r2[j - 2])
        sLD.inc(sync.dma_start(out=XQI[j % 2][:],
                               in_=xq2s[:, 128 * j:128 * (j + 1)].rearrange("(g p) t -> p g t", p=128)), 16)
        sLD.inc(sync.dma_start(out=x2t[j % 2][:], in_=x2s[128 * j:128 * (j + 1), :]), 16)
        ld[j] = sLD.v

    loads(0)
    loads(1)
    for j in range(NT):
        par = j % 2
        sLD.wait(tens, ld[j])
        if j == 0:
            sLD.wait(tens, ld_w)
        if j >= 2:
            sV.wait(tens, ev_r2[j - 2])
        b = pO2[par]
        for hc in range(32):
            for hf in range(2):
                mi = tens.matmul(out=b[:, 512 * hf:512 * (hf + 1)], lhsT=XQI[par][:, hc, :],
                            rhs=WQ2T[:, hc, 512 * hf:512 * (hf + 1)],
                            start=(hc == 0), stop=(hc == 31))
        mm_done[j] = sT.inc(mi)
        vec.tensor_tensor(out=st_[:, 9:10], in0=M2all[:, j:j + 1], in1=g_mw2[:], op=ALU.mult)
        vec.drain()
        sT.wait(vec, mm_done[j])
        if j >= 2:
            sS.wait(vec, ev_sq[j - 2])
            sST.wait(vec, ev_outdma[j - 2])
        ev_r2[j] = sV.inc(vec.scalar_tensor_tensor(out=rW[par][:], in0=b[:],
                                                   scalar=st_[:, 9:10], in1=x2t[par][:],
                                                   op0=ALU.mult, op1=ALU.add))
        vec.reduce_sum(out=st_[:, 0:1], in_=rW[par][:], axis=AX.X)
        vec.drain()
        sV.wait(scal, ev_r2[j])
        ev_sq[j] = sS.inc(scal.activation(out=trash[:], in_=rW[par][:], func=AF.Square, accum_out=st_[:, 1:2]))
        sS.wait(vec, ev_sq[j])
        vec.tensor_scalar(out=st_[:, 2:3], in0=st_[:, 0:1], scalar1=1.0 / D, scalar2=None, op0=ALU.mult)
        vec.tensor_scalar(out=st_[:, 3:4], in0=st_[:, 1:2], scalar1=1.0 / D, scalar2=None, op0=ALU.mult)
        vec.drain()
        vec.scalar_tensor_tensor(out=st_[:, 4:5], in0=st_[:, 2:3], scalar=st_[:, 2:3],
                                 in1=st_[:, 3:4], op0=ALU.mult, op1=ALU.subtract)
        vec.drain()
        vec.tensor_scalar(out=st_[:, 5:6], in0=st_[:, 4:5], scalar1=-1.0, scalar2=EPS,
                          op0=ALU.mult, op1=ALU.add)
        vec.drain()
        ev = sV.inc(vec.reciprocal(out=st_[:, 6:7], in_=st_[:, 5:6]))
        sV.wait(scal, ev)
        ev = sS.inc(scal.activation(out=st_[:, 7:8], in_=st_[:, 6:7], func=AF.Sqrt))
        sS.wait(vec, ev)
        vec.tensor_tensor(out=st_[:, 8:9], in0=st_[:, 2:3], in1=st_[:, 7:8], op=ALU.mult)
        vec.drain()
        vec.tensor_scalar(out=st_[:, 8:9], in0=st_[:, 8:9], scalar1=-1.0, scalar2=None, op0=ALU.mult)
        vec.drain()
        ev_out = sV.inc(vec.tensor_scalar(out=outt[par][:], in0=rW[par][:], scalar1=st_[:, 7:8],
                                          scalar2=st_[:, 8:9], op0=ALU.mult, op1=ALU.add))
        sV.wait(scal, ev_out)
        ev_outdma[j] = sST.inc(scal.dma_start(out=out_o[s, 128 * j:128 * (j + 1), :],
                                              in_=outt[par][:]), 16)
        if j + 2 < NT:
            loads(j + 2)

    sST.wait(scal)
    sLD.wait(sync)
    ctx.close()


_NC_CACHE = None


def _get_nc():
    global _NC_CACHE
    if _NC_CACHE is None:
        _NC_CACHE = build_nc()
    return _NC_CACHE


def kernel(L0_attn_X, L0_attn_att_mask, L0_attn_S_n, L0_attn_W_Q, L0_attn_W_K,
           L0_attn_W_V, L0_ffn_w1, L0_ffn_conv_w, L0_ffn_conv_b, L0_ffn_w2):
    X = np.ascontiguousarray(np.asarray(L0_attn_X, dtype=np.float32))
    WQ = np.ascontiguousarray(np.asarray(L0_attn_W_Q, dtype=np.float32))
    WK = np.ascontiguousarray(np.asarray(L0_attn_W_K, dtype=np.float32))
    WV = np.ascontiguousarray(np.asarray(L0_attn_W_V, dtype=np.float32))
    w1 = np.ascontiguousarray(np.asarray(L0_ffn_w1, dtype=np.float32))
    w2 = np.ascontiguousarray(np.asarray(L0_ffn_w2, dtype=np.float32))
    cw = np.asarray(L0_ffn_conv_w, dtype=np.float32)
    cb = np.asarray(L0_ffn_conv_b, dtype=np.float32)
    cwm = np.ascontiguousarray(cw[:, 0, 0].reshape(32, 128).T)
    cw0 = np.ascontiguousarray(cw[:, 0, 1].reshape(32, 128).T)
    cwp = np.ascontiguousarray(cw[:, 0, 2].reshape(32, 128).T)
    cbt = np.ascontiguousarray(cb.reshape(32, 128).T)

    nc = _get_nc()
    in_maps = []
    for c in range(8):
        in_maps.append(dict(
            x_in=np.ascontiguousarray(X[2 * c:2 * c + 2]),
            wq_in=WQ, wk_in=WK, wv_in=WV, w1_in=w1, w2_in=w2,
            cwm_in=cwm, cw0_in=cw0, cwp_in=cwp, cb_in=cbt,
        ))
    res = run_bass_kernel_spmd(nc, in_maps, core_ids=list(range(8)))
    out = np.concatenate([r["out_o"] for r in res.results], axis=0)
    return np.ascontiguousarray(out.astype(np.float32))
